# version 04
# baseline (speedup 1.0000x reference)
import numpy as np
import ml_dtypes
from contextlib import ExitStack

import concourse.bass as bass
import concourse.mybir as mybir
from concourse.bass_utils import run_bass_kernel_spmd

F32 = mybir.dt.float32
BF16 = mybir.dt.bfloat16
AF = mybir.ActivationFunctionType
ALU = mybir.AluOpType
AX = mybir.AxisListType

ENGINES = ("sync", "tensor", "vector", "scalar", "gpsimd")
DBG = set()
NEG = -1.0e30


class Sem:
    __slots__ = ("h", "n", "step")

    def __init__(self, h, step):
        self.h = h
        self.n = 0
        self.step = step

    def bump(self):
        self.n += self.step
        return self.n


class Plan:
    AUTO = ("vector", "scalar", "gpsimd")

    def __init__(self):
        self.q = {e: [] for e in ENGINES}
        self.esem = {}

    def emit(self, eng, fn, inc=None, wait=(), dma=False):
        w = list((s, v) for (s, v) in wait if s is not None and v)
        auto = eng in self.AUTO and not dma
        selfwait = None
        own = self.esem.get(eng)
        if auto and own is not None:
            if own.n:
                selfwait = (own, own.n)
            if inc is None:
                inc = own
        if selfwait is not None:
            w.append(selfwait)
        w = tuple(w)
        val = inc.bump() if inc is not None else None
        extra = None
        if auto and own is not None and inc is not own:
            extra = (own, own.bump())

        def run(E, fn=fn, w=w, inc=inc, extra=extra):
            for s, v in w:
                E.wait_ge(s.h, v)
            ins = fn(E)
            if inc is not None:
                ins = ins.then_inc(inc.h, inc.step)
            if extra is not None:
                ins.then_inc(extra[0].h, extra[0].step)
        self.q[eng].append(run)
        return val

    def waits(self, eng, *wait):
        w = tuple((s, v) for (s, v) in wait if s is not None and v)
        if not w:
            return

        def run(E, w=w):
            for s, v in w:
                E.wait_ge(s.h, v)
        self.q[eng].append(run)


def build(T=4096, seq=None, hammer=''):
    D = 1024
    nT8 = T // 1024
    nTSB = T // 512
    assert T % 1024 == 0
    if seq is None:
        seq = ["f0", "a0", "f0", "f1", "a1", "f1"]

    nc = bass.Bass("TRN2", target_bir_lowering=False, detect_race_conditions=False)

    x0 = nc.dram_tensor("x0", [T, D], F32, kind="ExternalInput").ap()
    wd = {}
    for name, shape in [
        ("L0_attn_Wq", [D, D]), ("L0_attn_Wk", [D, D]), ("L0_attn_Wv", [D, D]),
        ("L0_attn_Wo", [D, D]), ("L0_ffn_router", [D, 8]),
        ("L0_ffn_experts_gate", [8, D, 512]), ("L0_ffn_experts_up", [8, D, 512]),
        ("L0_ffn_experts_down", [8, 512, D]), ("L0_ffn_shared_gate", [D, D]),
        ("L0_ffn_shared_up", [D, D]), ("L0_ffn_shared_down", [D, D]),
        ("L1_attn_Wq", [D, D]), ("L1_attn_Wkv", [D, 128]), ("L1_attn_Wk", [128, 512]),
        ("L1_attn_Wv", [128, 512]), ("L1_attn_Wo", [D, D]),
        ("L1_attn_q_g", [64]), ("L1_attn_k_g", [32]),
        ("L1_ffn_router_w", [D, 14]), ("L1_ffn_w1", [14, D, 2048]),
        ("L1_ffn_b1", [14, 2048]), ("L1_ffn_w2", [14, 2048, D]),
        ("L1_ffn_b2", [14, D]),
    ]:
        wd[name] = nc.dram_tensor(name, shape, F32, kind="ExternalInput").ap()
    ident_bf_d = nc.dram_tensor("ident_bf", [128, 128], BF16, kind="ExternalInput").ap()
    ident_f_d = nc.dram_tensor("ident_f", [128, 128], F32, kind="ExternalInput").ap()
    cmask_d = nc.dram_tensor("cmask", [128, 2048], F32, kind="ExternalInput").ap()

    out_d = nc.dram_tensor("out", [T, D], F32, kind="ExternalOutput").ap()
    xa = nc.dram_tensor("xa", [T, D], F32, kind="Internal").ap()
    xb = nc.dram_tensor("xb", [T, D], F32, kind="Internal").ap()
    qspill = nc.dram_tensor("qspill", [8, 128, T], BF16, kind="Internal").ap()
    qrspill = nc.dram_tensor("qrspill", [4, 128, T], BF16, kind="Internal").ap()
    wcaches = {w: nc.dram_tensor(w, [8, 128, D], BF16, kind="Internal").ap()
               for w in ("wq_c", "wk_c", "wv_c", "wo_c")}
    c_l1w1 = nc.dram_tensor("c_l1w1", [14, 8, 128, 2048], BF16, kind="Internal").ap()
    c_l1w2 = nc.dram_tensor("c_l1w2", [14, 16, 128, 1024], BF16, kind="Internal").ap()
    c_l0w1 = nc.dram_tensor("c_l0w1", [8, 8, 128, 1024], BF16, kind="Internal").ap()
    c_l0w2 = nc.dram_tensor("c_l0w2", [8, 4, 128, 1024], BF16, kind="Internal").ap()
    c_sh1 = nc.dram_tensor("c_sh1", [8, 128, 2048], BF16, kind="Internal").ap()
    c_sh2 = nc.dram_tensor("c_sh2", [8, 128, 1024], BF16, kind="Internal").ap()

    P = Plan()
    _nctr = [0]

    def nm(p="t"):
        _nctr[0] += 1
        return f"_{p}{_nctr[0]}"

    with ExitStack() as top:
        TE = top.enter_context
        S = {
            "lx": Sem(TE(nc.semaphore("s_lx")), 16),
            "lw": Sem(TE(nc.semaphore("s_lw")), 16),
            "lw2": Sem(TE(nc.semaphore("s_lw2")), 16),
            "lq": Sem(TE(nc.semaphore("s_lq")), 16),
            "st": Sem(TE(nc.semaphore("s_st")), 16),
            "pe": Sem(TE(nc.semaphore("s_pe")), 1),
            "act": Sem(TE(nc.semaphore("s_act")), 1),
            "dve": Sem(TE(nc.semaphore("s_dve")), 1),
            "pool": Sem(TE(nc.semaphore("s_pool")), 1),
        }
        ENG_SEM = {"tensor": S["pe"], "vector": S["dve"], "scalar": S["act"],
                   "gpsimd": S["pool"]}
        P.esem = {"vector": S["dve"], "scalar": S["act"], "gpsimd": S["pool"]}

        def barrier():
            vals = []
            for e in ("tensor", "vector", "scalar", "gpsimd"):
                v = P.emit(e, lambda E: E.nop(), inc=ENG_SEM[e])
                vals.append((e, v))
            for e in ENGINES:
                P.waits(e, *[(ENG_SEM[o], v) for o, v in vals if o != e])
            P.waits("sync", (S["st"], S["st"].n))

        ident_bf = TE(nc.sbuf_tensor("ident_bf_s", [128, 128], BF16))
        ident_f = TE(nc.sbuf_tensor("ident_f_s", [128, 128], F32))
        ones_t = TE(nc.sbuf_tensor("ones_t", [128, 128], BF16))
        b1T = TE(nc.sbuf_tensor("b1T", [128, 224], F32))
        b2p = TE(nc.sbuf_tensor("b2p", [128, 4096], BF16))
        qgb = TE(nc.sbuf_tensor("qgb", [128, 1024], F32))
        kgb = TE(nc.sbuf_tensor("kgb", [128, 512], F32))
        cst = TE(nc.sbuf_tensor("cst", [128, 2], F32))
        nc.const_aps.register(F32, 1e-5, cst[:, 0:1])
        nc.const_aps.register(F32, 1e-6, cst[:, 1:2])

        c0 = P.emit("sync", lambda E: E.dma_start(out=ident_bf[:], in_=ident_bf_d), inc=S["lx"])
        c1 = P.emit("sync", lambda E: E.dma_start(out=ident_f[:], in_=ident_f_d), inc=S["lx"])
        P.emit("vector", lambda E: E.memset(ones_t[:], 1.0), inc=S["dve"])
        P.emit("vector", lambda E: E.memset(cst[:, 0:1], 1e-5))
        P.emit("vector", lambda E: E.memset(cst[:, 1:2], 1e-6), inc=S["dve"])
        P.waits("tensor", (S["lx"], c1), (S["dve"], S["dve"].n))
        P.waits("gpsimd", (S["lx"], c1))
        P.waits("scalar", (S["lx"], c1))

        with ExitStack() as prep:
            PE_ = prep.enter_context
            stg = PE_(nc.sbuf_tensor("p_stg", [128, 4096], F32))
            psA = PE_(nc.psum_tensor("p_psA", [128, 512], F32))
            psB = PE_(nc.psum_tensor("p_psB", [128, 512], F32))
            ptA = PE_(nc.psum_tensor("p_ptA", [128, 128], BF16))
            ptB = PE_(nc.psum_tensor("p_ptB", [128, 128], BF16))
            pt14 = PE_(nc.psum_tensor("p_pt14", [128, 28], F32))

            if "f1" in seq:
                b1s = PE_(nc.sbuf_tensor("p_b1s", [14, 2048], F32))
                v = P.emit("sync", lambda E: E.dma_start(out=b1s[:], in_=wd["L1_ffn_b1"]), inc=S["lw"])
                bsv = None
                for e in range(14):
                    bsv = P.emit("sync", lambda E, e=e: E.dma_start(
                        out=stg[32 * (e % 4):32 * (e % 4) + 1, (e // 4) * 1024:(e // 4) * 1024 + 1024],
                        in_=wd["L1_ffn_b2"][e:e + 1, :]), inc=S["lw"])
                P.emit("gpsimd", lambda E: E.memset(b2p[:], 0.0), inc=S["pool"])
                P.emit("gpsimd", lambda E, e=e: E.tensor_copy(
                    out=b2p[0:1, 0:4096], in_=stg[0:1, 0:4096]), wait=[(S["lw"], bsv)])
                P.emit("gpsimd", lambda E: E.tensor_copy(
                    out=b2p[32:33, 0:4096], in_=stg[32:33, 0:4096]))
                P.emit("gpsimd", lambda E: E.tensor_copy(
                    out=b2p[64:65, 0:3072], in_=stg[64:65, 0:3072]))
                b2v = P.emit("gpsimd", lambda E: E.tensor_copy(
                    out=b2p[96:97, 0:3072], in_=stg[96:97, 0:3072]), inc=S["pool"])
                last_evac = None
                for fc in range(16):
                    slot = pt14[:, (fc % 2) * 14:(fc % 2) * 14 + 14]
                    wv = [(S["lw"], v), (S["lx"], c1)]
                    if last_evac and fc >= 2:
                        wv.append(last_evac)
                    tv = P.emit("tensor", lambda E, s=slot, fc=fc: E.transpose(
                        s, b1s[:, fc * 128:(fc + 1) * 128], ident_f[0:14, 0:14]),
                        inc=S["pe"], wait=wv)
                    ev = P.emit("scalar", lambda E, s=slot, fc=fc: E.copy(
                        out=b1T[:, fc * 14:(fc + 1) * 14], in_=s), inc=S["act"],
                        wait=[(S["pe"], tv)])
                    last_evac = (S["act"], ev)
            if "a1" in seq:
                qg1 = PE_(nc.sbuf_tensor("p_qg1", [1, 64], F32))
                kg1 = PE_(nc.sbuf_tensor("p_kg1", [1, 32], F32))
                qg1b = PE_(nc.sbuf_tensor("p_qg1b", [1, 64], BF16))
                kg1b = PE_(nc.sbuf_tensor("p_kg1b", [1, 32], BF16))
                v = P.emit("sync", lambda E: E.dma_start(out=qg1[:], in_=wd["L1_attn_q_g"][None, :]), inc=S["lw"])
                v2 = P.emit("sync", lambda E: E.dma_start(out=kg1[:], in_=wd["L1_attn_k_g"][None, :]), inc=S["lw"])
                cv = P.emit("gpsimd", lambda E: E.tensor_copy(out=qg1b[:], in_=qg1[:]),
                            inc=S["pool"], wait=[(S["lw"], v)])
                cv2 = P.emit("gpsimd", lambda E: E.tensor_copy(out=kg1b[:], in_=kg1[:]),
                             inc=S["pool"], wait=[(S["lw"], v2)])
                for half in range(2):
                    ps = (psA, psB)[half]
                    mv = None
                    for j in range(8):
                        mv = P.emit("tensor", lambda E, ps=ps, j=j: E.matmul(
                            ps[:, j * 64:(j + 1) * 64], ones_t[0:1, 0:128], qg1b[:],
                            start=True, stop=True), inc=S["pe"],
                            wait=[(S["pool"], cv)] if j == 0 else [])
                    P.emit("scalar", lambda E, ps=ps, half=half: E.copy(
                        out=qgb[:, half * 512:(half + 1) * 512], in_=ps[:]),
                        inc=S["act"], wait=[(S["pe"], mv)])
                mv = None
                wv = [(S["pool"], cv2), (S["act"], S["act"].n)]
                for h in range(16):
                    mv = P.emit("tensor", lambda E, h=h: E.matmul(
                        psA[:, h * 32:(h + 1) * 32], ones_t[0:1, 0:128], kg1b[:],
                        start=True, stop=True), inc=S["pe"],
                        wait=wv if h == 0 else [])
                P.emit("scalar", lambda E: E.copy(out=kgb[:], in_=psA[:]),
                       inc=S["act"], wait=[(S["pe"], mv)])
            if "a0" in seq:
                wtmp = PE_(nc.sbuf_tensor("p_wtmp", [128, 8192], BF16))
                wTst = PE_(nc.sbuf_tensor("p_wTst", [128, 8192], BF16))
                stg_free = [None, None]
                wtst_store = None
                wtmp_used = None
                for wname, cache in [("L0_attn_Wq", "wq_c"), ("L0_attn_Wk", "wk_c"),
                                     ("L0_attn_Wv", "wv_c"), ("L0_attn_Wo", "wo_c")]:
                    w_ap = wd[wname]
                    cast_vals = []
                    for j in range(4):
                        par = j % 2
                        dv = P.emit("sync", lambda E, w_ap=w_ap, j=j, par=par: E.dma_start(
                            out=stg[:, par * 2048:(par + 1) * 2048].rearrange("p (a q) -> p a q", q=1024),
                            in_=w_ap.rearrange("(a p) m -> p a m", p=128)[:, 2 * j:2 * j + 2, :]),
                            inc=S["lw"], wait=[stg_free[par]] if stg_free[par] else [])
                        cw = [(S["lw"], dv)]
                        if wtmp_used is not None and j == 0:
                            cw.append(wtmp_used)
                        cvv = P.emit("gpsimd", lambda E, j=j, par=par: E.tensor_copy(
                            out=wtmp[:, j * 2048:(j + 1) * 2048],
                            in_=stg[:, par * 2048:(par + 1) * 2048]),
                            inc=S["pool"], wait=cw)
                        stg_free[par] = (S["pool"], cvv)
                        cast_vals.append(cvv)
                    tguard = [None, None]
                    tv_last = None
                    for mc in range(8):
                        for ncc in range(8):
                            k = mc * 8 + ncc
                            slot = (ptA, ptB)[k % 2]
                            w_ = [(S["pool"], cast_vals[ncc // 2]), (S["lx"], c0)]
                            if tguard[k % 2]:
                                w_.append(tguard[k % 2])
                            if wtst_store is not None and k == 0:
                                w_.append(wtst_store)
                            tv = P.emit("tensor", lambda E, slot=slot, mc=mc, ncc=ncc: E.transpose(
                                slot[:], wtmp[:, ncc * 1024 + mc * 128:ncc * 1024 + (mc + 1) * 128],
                                ident_bf[:]), inc=S["pe"], wait=w_)
                            ev = P.emit("scalar", lambda E, slot=slot, mc=mc, ncc=ncc: E.copy(
                                out=wTst[:, mc * 1024 + ncc * 128:mc * 1024 + (ncc + 1) * 128],
                                in_=slot[:]), inc=S["act"], wait=[(S["pe"], tv)])
                            tguard[k % 2] = (S["act"], ev)
                            tv_last = tv
                    wtmp_used = (S["pe"], tv_last)
                    sv = P.emit("scalar", lambda E, cache=cache: E.dma_start(
                        out=wcaches[cache].rearrange("i p m -> p i m"),
                        in_=wTst[:].rearrange("p (i m) -> p i m", m=1024)), inc=S["st"],
                        wait=[(S["act"], S["act"].n)], dma=True)
                    wtst_store = (S["st"], sv)
        barrier()
        with ExitStack() as prep2:
            PE2 = prep2.enter_context
            stgc = PE2(nc.sbuf_tensor("c_stg", [128, 4096], F32))
            cbf = PE2(nc.sbuf_tensor("c_cbf", [128, 4096], BF16))
            jobs = []
            if "f1" in seq:
                for e in range(14):
                    w1s = wd["L1_ffn_w1"][e].rearrange("(a p) f -> p a f", p=128)
                    for dc in range(8):
                        jobs.append(([(w1s[:, dc, :], 0, 2048)], c_l1w1[e, dc], 2048))
                    w2s = wd["L1_ffn_w2"][e].rearrange("(a p) q -> p a q", p=128)
                    for j in range(8):
                        jobs.append(([(w2s[:, 2 * j:2 * j + 2, :], 0, (2, 1024))],
                                     c_l1w2[e, 2 * j:2 * j + 2].rearrange("a p q -> p a q"), 2048))
            if "f0" in seq:
                for e in range(8):
                    g_ = wd["L0_ffn_experts_gate"][e].rearrange("(a p) f -> p a f", p=128)
                    u_ = wd["L0_ffn_experts_up"][e].rearrange("(a p) f -> p a f", p=128)
                    d_ = wd["L0_ffn_experts_down"][e].rearrange("(a p) q -> p a q", p=128)
                    for dc in range(8):
                        jobs.append(([(g_[:, dc, :], 0, 512), (u_[:, dc, :], 512, 512)],
                                     c_l0w1[e, dc], 1024))
                    for j in range(2):
                        jobs.append(([(d_[:, 2 * j:2 * j + 2, :], 0, (2, 1024))],
                                     c_l0w2[e, 2 * j:2 * j + 2].rearrange("a p q -> p a q"), 2048))
                g_ = wd["L0_ffn_shared_gate"].rearrange("(a p) f -> p a f", p=128)
                u_ = wd["L0_ffn_shared_up"].rearrange("(a p) f -> p a f", p=128)
                d_ = wd["L0_ffn_shared_down"].rearrange("(a p) q -> p a q", p=128)
                for dc in range(8):
                    jobs.append(([(g_[:, dc, :], 0, 1024), (u_[:, dc, :], 1024, 1024)],
                                 c_sh1[dc], 2048))
                for j in range(4):
                    jobs.append(([(d_[:, 2 * j:2 * j + 2, :], 0, (2, 1024))],
                                 c_sh2[2 * j:2 * j + 2].rearrange("a p q -> p a q"), 2048))
            stgc_free = [None, None]
            cbf_free = [None, None]
            for ji, (srcs, dst, width) in enumerate(jobs):
                par = ji % 2
                dv = None
                for k, (src, off, w_) in enumerate(srcs):
                    sgv = stgc[:, par * 2048 + off:par * 2048 + off +
                               (w_ if isinstance(w_, int) else w_[0] * w_[1])]
                    if not isinstance(w_, int):
                        sgv = sgv.rearrange("p (a q) -> p a q", q=w_[1])
                    wv = [stgc_free[par]] if (stgc_free[par] and k == 0) else []
                    dv = P.emit("sync", lambda E, src=src, sgv=sgv: E.dma_start(
                        out=sgv, in_=src), inc=S["lw"], wait=wv)
                ceng = "vector" if ji % 2 == 0 else "scalar"
                if ceng == "vector":
                    cv = P.emit("vector", lambda E, par=par, width=width: E.tensor_copy(
                        out=cbf[:, par * 2048:par * 2048 + width],
                        in_=stgc[:, par * 2048:par * 2048 + width]),
                        inc=S["dve"],
                        wait=[(S["lw"], dv)] + ([cbf_free[par]] if cbf_free[par] else []))
                    cs = (S["dve"], cv)
                else:
                    cv = P.emit("scalar", lambda E, par=par, width=width: E.copy(
                        out=cbf[:, par * 2048:par * 2048 + width],
                        in_=stgc[:, par * 2048:par * 2048 + width]),
                        inc=S["act"],
                        wait=[(S["lw"], dv)] + ([cbf_free[par]] if cbf_free[par] else []))
                    cs = (S["act"], cv)
                stgc_free[par] = cs
                src_out = cbf[:, par * 2048:par * 2048 + width]
                if len(dst.shape) == 3:
                    src_out = src_out.rearrange("p (a q) -> p a q", q=dst.shape[2])
                ov = P.emit("sync", lambda E, dst=dst, src_out=src_out: E.dma_start(
                    out=dst, in_=src_out), inc=S["st"], wait=[cs])
                cbf_free[par] = (S["st"], ov)
        barrier()

        def make_ln(bufs):
            state = {"sq_done": [None, None], "lny_free": [None, None],
                     "tguard": bufs.get("tguard", [None, None]), "tctr": 0}

            def ln_tc(x_ready, yT_war=None):
                xbuf, lnx, lnsq, lny, stats, yT = (bufs["xbuf"], bufs["lnx"],
                                                   bufs["lnsq"], bufs["lny"],
                                                   bufs["stats"], bufs["yT"])
                pts = bufs["pt"]
                ytv = []
                last_x_read = None
                for tcc in range(8):
                    par = tcc % 2
                    o = par * 16
                    st_ = stats
                    xt = xbuf[:, tcc * 1024:(tcc + 1) * 1024]
                    P.emit("vector", lambda E, xt=xt, o=o, st_=st_: E.reduce_sum(
                        out=st_[:, o:o + 1], in_=xt, axis=AX.X),
                        wait=[x_ready] if tcc == 0 else [])
                    P.emit("vector", lambda E, o=o, st_=st_: E.tensor_scalar_mul(
                        st_[:, o + 1:o + 2], st_[:, o:o + 1], 1.0 / 1024))
                    wv = [state["sq_done"][par]] if state["sq_done"][par] else []
                    dv = P.emit("vector", lambda E, xt=xt, o=o, par=par, st_=st_: E.tensor_scalar(
                        out=lnx[:, par * 1024:(par + 1) * 1024], in0=xt,
                        scalar1=st_[:, o + 1:o + 2], scalar2=None, op0=ALU.subtract),
                        inc=S["dve"], wait=wv)
                    last_x_read = (S["dve"], dv)
                    sv = P.emit("scalar", lambda E, o=o, par=par, st_=st_: E.activation(
                        out=lnsq[:], in_=lnx[:, par * 1024:(par + 1) * 1024],
                        func=AF.Square, accum_out=st_[:, o + 2:o + 3]),
                        inc=S["act"], wait=[(S["dve"], dv)])
                    state["sq_done"][par] = (S["act"], sv)
                    P.emit("scalar", lambda E, o=o, st_=st_: E.activation(
                        out=st_[:, o + 3:o + 4], in_=st_[:, o + 2:o + 3],
                        func=AF.Ln, bias=1e-5, scale=1.0 / 1024))
                    ev = P.emit("scalar", lambda E, o=o, st_=st_: E.activation(
                        out=st_[:, o + 4:o + 5], in_=st_[:, o + 3:o + 4],
                        func=AF.Exp, scale=-0.5), inc=S["act"])
                    wv = [(S["act"], ev)]
                    if state["lny_free"][par]:
                        wv.append(state["lny_free"][par])
                    yv = P.emit("vector", lambda E, o=o, par=par, st_=st_: E.tensor_scalar(
                        out=lny[:, par * 1024:(par + 1) * 1024],
                        in0=lnx[:, par * 1024:(par + 1) * 1024],
                        scalar1=st_[:, o + 4:o + 5], scalar2=None, op0=ALU.mult),
                        inc=S["dve"], wait=wv)
                    last_ev = None
                    tv = None
                    for dc in range(8):
                        sl = state["tctr"] % 2
                        state["tctr"] += 1
                        wv = [(S["dve"], yv)] if dc == 0 else []
                        if state["tguard"][sl]:
                            wv.append(state["tguard"][sl])
                        if yT_war is not None and tcc == 0 and dc == 0:
                            wv.append(yT_war)
                        tv = P.emit("tensor", lambda E, sl=sl, dc=dc, par=par: E.transpose(
                            pts[sl][:],
                            lny[:, par * 1024 + dc * 128:par * 1024 + (dc + 1) * 128],
                            ident_bf[:]), inc=S["pe"], wait=wv)
                        ev2 = P.emit("scalar", lambda E, sl=sl, dc=dc, tcc=tcc: E.copy(
                            out=yT[:, dc * 1024 + tcc * 128:dc * 1024 + tcc * 128 + 128],
                            in_=pts[sl][:]), inc=S["act"], wait=[(S["pe"], tv)])
                        state["tguard"][sl] = (S["act"], ev2)
                        last_ev = ev2
                    state["lny_free"][par] = (S["pe"], tv)
                    ytv.append((S["act"], last_ev))
                return ytv, last_x_read
            return ln_tc

        def emit_ffn(layer, x_in, x_out):
            E_ = 8 if layer == "l0" else 14
            with ExitStack() as ph:
                A = ph.enter_context
                xbuf = A(nc.sbuf_tensor(nm(), [128, 8192], F32))
                lnx = A(nc.sbuf_tensor(nm(), [128, 2048], F32))
                lnsq = A(nc.sbuf_tensor(nm(), [128, 1024], F32))
                lny = A(nc.sbuf_tensor(nm(), [128, 2048], BF16))
                stats = A(nc.sbuf_tensor(nm(), [128, 32], F32))
                yT = A(nc.sbuf_tensor(nm(), [128, 8192], BF16))
                w1b = A(nc.sbuf_tensor(nm(), [128, 16384], BF16))
                w2b = A(nc.sbuf_tensor(nm(), [128, 16384], BF16))
                hT = A(nc.sbuf_tensor(nm(), [128, 16384], BF16))
                stg = A(nc.sbuf_tensor(nm(), [128, 4096], F32))
                gs = A(nc.sbuf_tensor(nm(), [128, 1024], F32))
                rt = A(nc.sbuf_tensor(nm(), [128, 128], F32))
                coefb = A(nc.sbuf_tensor(nm(), [128, 112], F32))
                rtrb = A(nc.sbuf_tensor(nm(), [128, 8 * 14], BF16))
                ps = [A(nc.psum_tensor(nm('q'), [128, 512], F32)) for _ in range(6)]
                pts = [A(nc.psum_tensor(nm('q'), [128, 128], BF16)) for _ in range(2)]
                bguard = {}
                ln_tc = make_ln({"xbuf": xbuf, "lnx": lnx, "lnsq": lnsq, "lny": lny,
                                 "stats": stats, "yT": yT, "pt": pts})

                rv = P.emit("sync", lambda E: E.dma_start(
                    out=stg[:, 0:8 * E_].rearrange("p (a q) -> p a q", q=E_),
                    in_=(wd["L0_ffn_router"] if layer == "l0" else wd["L1_ffn_router_w"])
                    .rearrange("(a p) e -> p a e", p=128)), inc=S["lw"])
                rcv = P.emit("gpsimd", lambda E: E.tensor_copy(
                    out=rtrb[:, 0:8 * E_], in_=stg[:, 0:8 * E_]),
                    inc=S["pool"], wait=[(S["lw"], rv)])
                stg_free = [(S["pool"], rcv), (S["pool"], rcv)]

                if layer == "l0":
                    experts = [("swiglu", 512, e) for e in range(8)] + [("swiglu", 1024, "sh")]
                else:
                    experts = [("gelu", 2048, e) for e in range(14)]

                s1_end_prev = None
                s2_end_prev = None
                hT_free = None
                store_prev = None
                evac_last = None
                for tc in range(nT8):
                    lxv = P.emit("sync", lambda E, tc=tc: E.dma_start(
                        out=xbuf[:].rearrange("p (j d) -> p j d", d=1024),
                        in_=x_in.rearrange("(a j p) d -> a p j d", p=128, j=8)[tc]),
                        inc=S["lx"], wait=[store_prev] if store_prev else [])
                    ytv, _ = ln_tc((S["lx"], lxv), yT_war=s1_end_prev if tc > 0 else None)
                    coef_ready = []
                    for tcc in range(8):
                        o = (tcc % 2) * 64
                        so = (tcc % 2) * 16
                        wv = [ytv[tcc], (S["pool"], rcv)]
                        if ("s", 0) in bguard:
                            wv.append(bguard[("s", 0)])
                        mv = None
                        for dc in range(8):
                            mv = P.emit("tensor", lambda E, dc=dc, tcc=tcc: E.matmul(
                                ps[0][:, 0:E_],
                                yT[:, dc * 1024 + tcc * 128:dc * 1024 + tcc * 128 + 128],
                                rtrb[:, dc * E_:(dc + 1) * E_],
                                start=(dc == 0), stop=(dc == 7)), inc=S["pe"],
                                wait=wv if dc == 0 else [])
                        nv = P.emit("vector", lambda E, so=so: E.tensor_reduce(
                            out=stats[:, so + 5:so + 6], in_=ps[0][:, 0:E_],
                            axis=AX.X, op=ALU.max, negate=True), inc=S["dve"],
                            wait=[(S["pe"], mv)])
                        rt_war = coef_ready[tcc - 2] if tcc >= 2 else None
                        ev = P.emit("scalar", lambda E, o=o, so=so: E.activation(
                            out=rt[:, o:o + E_], in_=ps[0][:, 0:E_], func=AF.Exp,
                            bias=stats[:, so + 5:so + 6]), inc=S["act"],
                            wait=[(S["dve"], nv)] + ([rt_war] if rt_war else []))
                        bguard[("s", 0)] = (S["act"], ev)
                        pr = rt[:, o:o + E_]
                        t1 = rt[:, o + 16:o + 16 + E_]
                        t2 = rt[:, o + 32:o + 32 + E_]
                        P.emit("vector", lambda E, pr=pr, so=so: E.reduce_sum(
                            out=stats[:, so + 11:so + 12], in_=pr, axis=AX.X),
                            wait=[(S["act"], ev)])
                        P.emit("vector", lambda E, so=so: E.reciprocal(
                            out=stats[:, so + 10:so + 11], in_=stats[:, so + 11:so + 12]))
                        P.emit("vector", lambda E, pr=pr, so=so: E.tensor_scalar(
                            out=pr, in0=pr, scalar1=stats[:, so + 10:so + 11],
                            scalar2=None, op0=ALU.mult))
                        P.emit("vector", lambda E, pr=pr, so=so: E.tensor_reduce(
                            out=stats[:, so + 6:so + 7], in_=pr, axis=AX.X, op=ALU.max))
                        P.emit("vector", lambda E, pr=pr, t1=t1, so=so: E.tensor_scalar(
                            out=t1, in0=pr, scalar1=stats[:, so + 6:so + 7],
                            scalar2=None, op0=ALU.is_ge))
                        P.emit("vector", lambda E, t1=t1, so=so: E.reduce_sum(
                            out=stats[:, so + 7:so + 8], in_=t1, axis=AX.X))
                        if layer == "l0":
                            P.emit("vector", lambda E, pr=pr, t1=t1, t2=t2: E.scalar_tensor_tensor(
                                out=t2, in0=t1, scalar=NEG, in1=pr, op0=ALU.mult, op1=ALU.add))
                            P.emit("vector", lambda E, t2=t2, so=so: E.tensor_reduce(
                                out=stats[:, so + 8:so + 9], in_=t2, axis=AX.X, op=ALU.max))
                            P.emit("vector", lambda E, t2=t2, so=so: E.tensor_scalar(
                                out=t2, in0=t2, scalar1=stats[:, so + 8:so + 9],
                                scalar2=None, op0=ALU.is_ge))
                            P.emit("vector", lambda E, so=so: E.tensor_scalar(
                                out=stats[:, so + 9:so + 10], in0=stats[:, so + 7:so + 8],
                                scalar1=1.5, scalar2=None, op0=ALU.is_lt))
                            P.emit("vector", lambda E, t2=t2, so=so: E.tensor_scalar(
                                out=t2, in0=t2, scalar1=stats[:, so + 9:so + 10],
                                scalar2=None, op0=ALU.mult))
                            P.emit("vector", lambda E, t1=t1, t2=t2: E.tensor_tensor(
                                out=t1, in0=t1, in1=t2, op=ALU.add))
                            P.emit("vector", lambda E, t1=t1, pr=pr: E.tensor_tensor(
                                out=t1, in0=t1, in1=pr, op=ALU.mult))
                        else:
                            P.emit("vector", lambda E, so=so: E.reciprocal(
                                out=stats[:, so + 9:so + 10], in_=stats[:, so + 7:so + 8]))
                            P.emit("vector", lambda E, so=so: E.tensor_tensor(
                                out=stats[:, so + 8:so + 9], in0=stats[:, so + 6:so + 7],
                                in1=stats[:, so + 9:so + 10], op=ALU.mult))
                            P.emit("vector", lambda E, t1=t1, so=so: E.tensor_scalar(
                                out=t1, in0=t1, scalar1=stats[:, so + 8:so + 9],
                                scalar2=None, op0=ALU.mult))
                        cv = P.emit("vector", lambda E, t1=t1, tcc=tcc: E.tensor_scalar(
                            out=coefb[:, tcc * E_:(tcc + 1) * E_], in0=t1, scalar1=0.5,
                            scalar2=None, op0=ALU.mult), inc=S["dve"])
                        coef_ready.append((S["dve"], cv))

                    for kind, F_, e in experts:
                        nfc1 = F_ // 128
                        nfc2 = F_ // 128
                        if layer == "l1":
                            w1src = c_l1w1[e].rearrange("a p f -> p a f")
                            w2src = c_l1w2[e].rearrange("a p q -> p a q")
                            w1dst = w1b[:].rearrange("p (a q) -> p a q", q=2048)
                            w2dst = w2b[:].rearrange("p (a q) -> p a q", q=1024)
                        elif e == "sh":
                            w1src = c_sh1.rearrange("a p f -> p a f")
                            w2src = c_sh2.rearrange("a p q -> p a q")
                            w1dst = w1b[:].rearrange("p (a q) -> p a q", q=2048)
                            w2dst = w2b[:, 0:8 * 1024].rearrange("p (a q) -> p a q", q=1024)
                        else:
                            w1src = c_l0w1[e].rearrange("a p f -> p a f")
                            w2src = c_l0w2[e].rearrange("a p q -> p a q")
                            w1dst = w1b[:, 0:8 * 1024].rearrange("p (a q) -> p a q", q=1024)
                            w2dst = w2b[:, 0:4 * 1024].rearrange("p (a q) -> p a q", q=1024)
                        wv1 = [s1_end_prev] if s1_end_prev else []
                        dv1 = P.emit("sync", lambda E, w1src=w1src, w1dst=w1dst: E.dma_start(
                            out=w1dst, in_=w1src), inc=S["lw"], wait=wv1)
                        w1_ready = (S["lw"], dv1)
                        wv2 = [s2_end_prev] if s2_end_prev else []
                        dv2 = P.emit("sync", lambda E, w2src=w2src, w2dst=w2dst: E.dma_start(
                            out=w2dst, in_=w2src), inc=S["lw2"], wait=wv2)
                        w2_ready = (S["lw2"], dv2)

                        cnt = 0
                        s1_last = None
                        h_last = None
                        for fc in range(nfc1):
                            for th in range(2):
                                if kind == "gelu":
                                    bank = cnt % 2
                                    pp = ps[bank]
                                    wv = []
                                    if cnt == 0:
                                        wv = [w1_ready, ytv[-1]]
                                        if hT_free is not None:
                                            wv.append(hT_free)
                                    if ("s", bank) in bguard:
                                        wv.append(bguard[("s", bank)])
                                    mv = None
                                    for dc in range(8):
                                        mv = P.emit("tensor", lambda E, pp=pp, dc=dc, fc=fc, th=th: E.matmul(
                                            pp[:],
                                            w1b[:, dc * 2048 + fc * 128:dc * 2048 + (fc + 1) * 128],
                                            yT[:, dc * 1024 + th * 512:dc * 1024 + (th + 1) * 512],
                                            start=(dc == 0), stop=(dc == 7)), inc=S["pe"],
                                            wait=wv if dc == 0 else [])
                                    s1_last = mv
                                    ev = P.emit("scalar", lambda E, pp=pp, fc=fc, th=th, e=e: E.activation(
                                        out=hT[:, fc * 1024 + th * 512:fc * 1024 + (th + 1) * 512],
                                        in_=pp[:], func=AF.Gelu,
                                        bias=b1T[:, fc * 14 + e:fc * 14 + e + 1]),
                                        inc=S["act"], wait=[(S["pe"], mv)])
                                    bguard[("s", bank)] = (S["act"], ev)
                                    h_last = (S["act"], ev)
                                else:
                                    bg, bu = (0, 1) if cnt % 2 == 0 else (4, 5)
                                    ppg, ppu = ps[bg], ps[bu]
                                    qw = 2048 if e == "sh" else 1024
                                    goff, uoff = 0, (1024 if e == "sh" else 512)
                                    wv = []
                                    if cnt == 0:
                                        wv = [w1_ready, ytv[-1]]
                                        if hT_free is not None:
                                            wv.append(hT_free)
                                    if ("s", bg) in bguard:
                                        wv.append(bguard[("s", bg)])
                                    mgv = None
                                    for dc in range(8):
                                        mgv = P.emit("tensor", lambda E, ppg=ppg, dc=dc, fc=fc, th=th, qw=qw, goff=goff: E.matmul(
                                            ppg[:],
                                            w1b[:, dc * qw + goff + fc * 128:dc * qw + goff + (fc + 1) * 128],
                                            yT[:, dc * 1024 + th * 512:dc * 1024 + (th + 1) * 512],
                                            start=(dc == 0), stop=(dc == 7)), inc=S["pe"],
                                            wait=wv if dc == 0 else [])
                                    wv2 = []
                                    if ("s", bu) in bguard:
                                        wv2.append(bguard[("s", bu)])
                                    muv = None
                                    for dc in range(8):
                                        muv = P.emit("tensor", lambda E, ppu=ppu, dc=dc, fc=fc, th=th, qw=qw, uoff=uoff: E.matmul(
                                            ppu[:],
                                            w1b[:, dc * qw + uoff + fc * 128:dc * qw + uoff + (fc + 1) * 128],
                                            yT[:, dc * 1024 + th * 512:dc * 1024 + (th + 1) * 512],
                                            start=(dc == 0), stop=(dc == 7)), inc=S["pe"],
                                            wait=wv2 if dc == 0 else [])
                                    s1_last = muv
                                    gpar = cnt % 2
                                    sv = P.emit("scalar", lambda E, ppg=ppg, gpar=gpar: E.activation(
                                        out=gs[:, gpar * 512:(gpar + 1) * 512], in_=ppg[:],
                                        func=AF.Silu), inc=S["act"], wait=[(S["pe"], mgv)])
                                    bguard[("s", bg)] = (S["act"], sv)
                                    hv = P.emit("vector", lambda E, ppu=ppu, gpar=gpar, fc=fc, th=th: E.tensor_tensor(
                                        out=hT[:, fc * 1024 + th * 512:fc * 1024 + (th + 1) * 512],
                                        in0=gs[:, gpar * 512:(gpar + 1) * 512], in1=ppu[:],
                                        op=ALU.mult), inc=S["dve"],
                                        wait=[(S["act"], sv), (S["pe"], muv)])
                                    bguard[("s", bu)] = (S["dve"], hv)
                                    h_last = (S["dve"], hv)
                                cnt += 1
                        s1_end_prev = (S["pe"], s1_last)

                        s2_last = None
                        for tcc in range(8):
                            for half in range(2):
                                bank = 2 + half
                                pp = ps[bank]
                                wv = [h_last]
                                if w2_ready is not None and tcc == 0 and half == 0:
                                    wv.append(w2_ready)
                                if ("s", bank) in bguard:
                                    wv.append(bguard[("s", bank)])
                                mv = None
                                for fc in range(nfc2):
                                    mv = P.emit("tensor", lambda E, pp=pp, fc=fc, tcc=tcc, half=half, nfc2=nfc2, layer=layer: E.matmul(
                                        pp[:],
                                        hT[:, fc * 1024 + tcc * 128:fc * 1024 + (tcc + 1) * 128],
                                        w2b[:, fc * 1024 + half * 512:fc * 1024 + (half + 1) * 512],
                                        start=(fc == 0), stop=(fc == nfc2 - 1 and layer != "l1")),
                                        inc=S["pe"], wait=wv if fc == 0 else [])
                                if layer == "l1":
                                    bb = 32 * (e % 4)
                                    bcol = (e // 4) * 1024 + half * 512
                                    mv = P.emit("tensor", lambda E, pp=pp, bb=bb, bcol=bcol: E.matmul(
                                        pp[:], ones_t[bb:bb + 1, 0:128],
                                        b2p[bb:bb + 1, bcol:bcol + 512],
                                        start=False, stop=True, tile_position=(bb, 0)),
                                        inc=S["pe"])
                                s2_last = mv
                                scal = 0.5 if e == "sh" else coefb[:, tcc * E_ + e:tcc * E_ + e + 1]
                                xsl = xbuf[:, tcc * 1024 + half * 512:tcc * 1024 + (half + 1) * 512]
                                dvv = P.emit("vector", lambda E, pp=pp, scal=scal, xsl=xsl: E.scalar_tensor_tensor(
                                    out=xsl, in0=pp[:], scalar=scal, in1=xsl,
                                    op0=ALU.mult, op1=ALU.add), inc=S["dve"],
                                    wait=[(S["pe"], mv), coef_ready[tcc]])
                                bguard[("s", bank)] = (S["dve"], dvv)
                                evac_last = (S["dve"], dvv)
                        s2_end_prev = (S["pe"], s2_last)
                        hT_free = (S["pe"], s2_last)

                    sv = P.emit("scalar", lambda E, tc=tc: E.dma_start(
                        out=x_out.rearrange("(a j p) d -> a p j d", p=128, j=8)[tc],
                        in_=xbuf[:].rearrange("p (j d) -> p j d", d=1024)),
                        inc=S["st"], wait=[evac_last], dma=True)
                    store_prev = (S["st"], sv)

        def emit_attn(layer, x_in, x_out):
            is0 = layer == "l0"
            HPG = 8 if is0 else 16
            NG = 2 if is0 else 1
            HD = 64 if is0 else 32
            VW = 64 if is0 else 32
            VS = VW + 1
            vs_tile = HPG * VS
            GW = HPG * VW
            with ExitStack() as ph:
                A = ph.enter_context
                kt = [A(nc.sbuf_tensor(nm(), [128, T], BF16)) for _ in range(4)]
                vt = A(nc.sbuf_tensor(nm(), [128, (T // 128) * vs_tile], BF16))
                stats = A(nc.sbuf_tensor(nm(), [128, 32], F32))
                wot_g = A(nc.sbuf_tensor(nm(), [128, 4096], BF16))
                stg = A(nc.sbuf_tensor(nm(), [128, 2048], F32))

                if not is0:
                    wq_b = A(nc.sbuf_tensor(nm(), [128, 8192], BF16))
                    wkv_b = A(nc.sbuf_tensor(nm(), [128, 1024], BF16))
                    wk_b = A(nc.sbuf_tensor(nm(), [128, 512], BF16))
                    wv_b = A(nc.sbuf_tensor(nm(), [128, 512], BF16))
                    sfree = None
                    jobs = []
                    wqs = wd["L1_attn_Wq"].rearrange("(a p) n -> p a n", p=128)
                    for j in range(8):
                        jobs.append((wqs[:, j, :], wq_b[:, j * 1024:(j + 1) * 1024], None, 1024))
                    jobs.append((wd["L1_attn_Wkv"].rearrange("(a p) l -> p a l", p=128),
                                 wkv_b[:].rearrange("p (a l) -> p a l", l=128), 128, 1024))
                    jobs.append((wd["L1_attn_Wk"], wk_b[:], None, 512))
                    jobs.append((wd["L1_attn_Wv"], wv_b[:], None, 512))
                    for src, dst, q, width in jobs:
                        sgv = stg[:, 0:width]
                        if q is not None:
                            sgv = sgv.rearrange("p (a q) -> p a q", q=q)
                        dv = P.emit("sync", lambda E, src=src, sgv=sgv: E.dma_start(
                            out=sgv, in_=src), inc=S["lw"],
                            wait=[sfree] if sfree else [])
                        sin = stg[:, 0:width]
                        if q is not None:
                            sin = sin.rearrange("p (a q) -> p a q", q=q)
                        cvv = P.emit("gpsimd", lambda E, sin=sin, dst=dst: E.tensor_copy(
                            out=dst, in_=sin), inc=S["pool"], wait=[(S["lw"], dv)])
                        sfree = (S["pool"], cvv)
                    wq_ready = sfree
                    for oc in range(4):
                        dvs = None
                        for hh in range(4):
                            h = oc * 4 + hh
                            dvs = P.emit("sync", lambda E, h=h, hh=hh: E.dma_start(
                                out=stg[32 * hh:32 * hh + 32, 0:1024],
                                in_=wd["L1_attn_Wo"][h * 64:h * 64 + 32, :]), inc=S["lw"],
                                wait=[sfree] if hh == 0 else [])
                        cvv = P.emit("gpsimd", lambda E, oc=oc: E.tensor_copy(
                            out=wot_g[:, oc * 1024:(oc + 1) * 1024],
                            in_=stg[:, 0:1024]), inc=S["pool"], wait=[(S["lw"], dvs)])
                        sfree = (S["pool"], cvv)
                    wo_ready = sfree

                def passA(g):
                    with ExitStack() as pa:
                        B_ = pa.enter_context
                        psS = [B_(nc.psum_tensor(nm('q'), [128, 512], F32)) for _ in range(2)]
                        psO = [B_(nc.psum_tensor(nm('q'), [128, 512], F32)) for _ in range(2)]
                        pstA = B_(nc.psum_tensor(nm('q'), [128, 128], BF16))
                        pstB = B_(nc.psum_tensor(nm('q'), [128, 128], BF16))
                        xbuf = B_(nc.sbuf_tensor(nm(), [128, 8192], F32))
                        lnx = B_(nc.sbuf_tensor(nm(), [128, 2048], F32))
                        lnsq = B_(nc.sbuf_tensor(nm(), [128, 1024], F32))
                        lny = B_(nc.sbuf_tensor(nm(), [128, 2048], BF16))
                        yT = B_(nc.sbuf_tensor(nm(), [128, 8192], BF16))
                        qst = B_(nc.sbuf_tensor(nm(), [128, 4096], BF16))
                        if is0:
                            wcur = B_(nc.sbuf_tensor(nm(), [128, 4096], BF16))
                        else:
                            kvT = B_(nc.sbuf_tensor(nm(), [128, 1024], BF16))
                            mtp0 = B_(nc.sbuf_tensor(nm(), [128, 512], F32))
                            qrope = B_(nc.sbuf_tensor(nm(), [128, 1024], BF16))
                            knb = B_(nc.sbuf_tensor(nm(), [128, 1024], BF16))
                            sqf = B_(nc.sbuf_tensor(nm(), [128, 512], F32))
                            rms = B_(nc.sbuf_tensor(nm(), [128, 96], F32))
                        bguard = {}
                        tg = [None, None]
                        ln_tc = make_ln({"xbuf": xbuf, "lnx": lnx, "lnsq": lnsq,
                                         "lny": lny, "stats": stats, "yT": yT,
                                         "pt": [pstA, pstB], "tguard": tg})
                        qst_free = None
                        x_read_prev = None
                        wcur_guard = None
                        for tc in range(nT8):
                            lxv = P.emit("sync", lambda E, tc=tc: E.dma_start(
                                out=xbuf[:].rearrange("p (j d) -> p j d", d=1024),
                                in_=x_in.rearrange("(a j p) d -> a p j d", p=128, j=8)[tc]),
                                inc=S["lx"], wait=[x_read_prev] if x_read_prev else [])
                            ytv, x_read_prev = ln_tc((S["lx"], lxv))
                            yT_all = ytv[-1]
                            if is0:
                                for which, cache in (("q", "wq_c"), ("k", "wk_c")):
                                    wlv = P.emit("sync", lambda E, cache=cache, g=g: E.dma_start(
                                        out=wcur[:].rearrange("p (i m) -> p i m", m=512),
                                        in_=wcaches[cache].rearrange("i p m -> p i m")[:, :, g * 512:(g + 1) * 512]),
                                        inc=S["lq"], wait=[wcur_guard] if wcur_guard else [])
                                    for dl in range(4):
                                        for th in range(2):
                                            bank = (dl * 2 + th) % 2
                                            pp = psS[bank]
                                            wv = [yT_all, (S["lq"], wlv)] if (dl == 0 and th == 0) else []
                                            if ("pj", bank) in bguard:
                                                wv.append(bguard[("pj", bank)])
                                            mv = None
                                            for mc in range(8):
                                                mv = P.emit("tensor", lambda E, pp=pp, mc=mc, dl=dl, th=th: E.matmul(
                                                    pp[:],
                                                    wcur[:, mc * 512 + dl * 128:mc * 512 + (dl + 1) * 128],
                                                    yT[:, mc * 1024 + th * 512:mc * 1024 + (th + 1) * 512],
                                                    start=(mc == 0), stop=(mc == 7)), inc=S["pe"],
                                                    wait=wv if mc == 0 else [])
                                            wcur_guard = (S["pe"], mv)
                                            if which == "q":
                                                wv2 = [(S["pe"], mv)]
                                                if qst_free is not None and dl == 0 and th == 0:
                                                    wv2.append(qst_free)
                                                ev = P.emit("scalar", lambda E, pp=pp, dl=dl, th=th: E.copy(
                                                    out=qst[:, dl * 1024 + th * 512:dl * 1024 + (th + 1) * 512],
                                                    in_=pp[:]), inc=S["act"], wait=wv2)
                                            else:
                                                ev = P.emit("scalar", lambda E, pp=pp, dl=dl, th=th, tc=tc: E.copy(
                                                    out=kt[dl][:, tc * 1024 + th * 512:tc * 1024 + (th + 1) * 512],
                                                    in_=pp[:]), inc=S["act"], wait=[(S["pe"], mv)])
                                            bguard[("pj", bank)] = (S["act"], ev)
                                    if which == "q":
                                        qsv = P.emit("scalar", lambda E, tc=tc, g=g: E.dma_start(
                                            out=qspill[g * 4:g * 4 + 4, :, tc * 1024:(tc + 1) * 1024]
                                            .rearrange("i p t -> p i t"),
                                            in_=qst[:].rearrange("p (i t) -> p i t", t=1024)),
                                            inc=S["st"], wait=[(S["act"], S["act"].n)], dma=True)
                                        qst_free = (S["st"], qsv)
                                wlv = P.emit("sync", lambda E, g=g: E.dma_start(
                                    out=wcur[:].rearrange("p (i m) -> p i m", m=512),
                                    in_=wcaches["wv_c"].rearrange("i p m -> p i m")[:, :, g * 512:(g + 1) * 512]),
                                    inc=S["lq"], wait=[wcur_guard] if wcur_guard else [])
                                for tcc in range(8):
                                    bank = tcc % 2
                                    pp = psO[bank]
                                    wv = [yT_all, (S["lq"], wlv)] if tcc == 0 else []
                                    if ("v", bank) in bguard:
                                        wv.append(bguard[("v", bank)])
                                    mv = None
                                    for mc in range(8):
                                        mv = P.emit("tensor", lambda E, pp=pp, mc=mc, tcc=tcc: E.matmul(
                                            pp[:],
                                            yT[:, mc * 1024 + tcc * 128:mc * 1024 + (tcc + 1) * 128],
                                            wcur[:, mc * 512:(mc + 1) * 512],
                                            start=(mc == 0), stop=(mc == 7)), inc=S["pe"],
                                            wait=wv if mc == 0 else [])
                                    wcur_guard = (S["pe"], mv)
                                    ti = tc * 8 + tcc
                                    P.emit("vector", lambda E, pp=pp, ti=ti: E.tensor_copy(
                                        out=vt[:, ti * vs_tile:(ti + 1) * vs_tile]
                                        .rearrange("p (h s) -> p h s", s=VS)[:, :, 0:VW],
                                        in_=pp[:].rearrange("p (h s) -> p h s", s=VW)),
                                        wait=[(S["pe"], mv)])
                                    vv = P.emit("vector", lambda E, ti=ti: E.memset(
                                        vt[:, ti * vs_tile:(ti + 1) * vs_tile]
                                        .rearrange("p (h s) -> p h s", s=VS)[:, :, VW:VS], 1.0),
                                        inc=S["dve"])
                                    bguard[("v", bank)] = (S["dve"], vv)
                            else:
                                kv_ready = None
                                for th in range(2):
                                    bank = th
                                    pp = psS[bank]
                                    wv = [yT_all, wq_ready] if th == 0 else []
                                    if ("kv", bank) in bguard:
                                        wv.append(bguard[("kv", bank)])
                                    mv = None
                                    for dc in range(8):
                                        mv = P.emit("tensor", lambda E, pp=pp, dc=dc, th=th: E.matmul(
                                            pp[:], wkv_b[:, dc * 128:(dc + 1) * 128],
                                            yT[:, dc * 1024 + th * 512:dc * 1024 + (th + 1) * 512],
                                            start=(dc == 0), stop=(dc == 7)), inc=S["pe"],
                                            wait=wv if dc == 0 else [])
                                    ev = P.emit("scalar", lambda E, pp=pp, th=th: E.copy(
                                        out=kvT[:, th * 512:(th + 1) * 512], in_=pp[:]),
                                        inc=S["act"], wait=[(S["pe"], mv)])
                                    bguard[("kv", bank)] = (S["act"], ev)
                                    kv_ready = (S["act"], ev)
                                for tcc in range(8):
                                    par = tcc % 2
                                    qdone = None
                                    for qh in range(2 if "pa_no_q" not in DBG else 0):
                                        bank = qh
                                        pp = psS[bank]
                                        wv = [kv_ready]
                                        if ("kv", bank) in bguard:
                                            wv.append(bguard[("kv", bank)])
                                        mv = None
                                        for dc in range(8):
                                            mv = P.emit("tensor", lambda E, pp=pp, dc=dc, tcc=tcc, qh=qh: E.matmul(
                                                pp[:],
                                                yT[:, dc * 1024 + tcc * 128:dc * 1024 + (tcc + 1) * 128],
                                                wq_b[:, dc * 1024 + qh * 512:dc * 1024 + (qh + 1) * 512],
                                                start=(dc == 0), stop=(dc == 7)), inc=S["pe"],
                                                wait=wv if dc == 0 else [])
                                        sv = P.emit("scalar", lambda E, pp=pp: E.activation(
                                            out=sqf[:], in_=pp[:], func=AF.Square),
                                            inc=S["act"], wait=[(S["pe"], mv)])
                                        rv = P.emit("vector", lambda E, qh=qh: E.tensor_reduce(
                                            out=rms[:, qh * 8:qh * 8 + 8].rearrange("p (h s) -> p h s", s=1),
                                            in_=sqf[:].rearrange("p (h s) -> p h s", s=64),
                                            axis=AX.X, op=ALU.add), inc=S["dve"],
                                            wait=[(S["act"], sv)])
                                        P.emit("scalar", lambda E, qh=qh: E.activation(
                                            out=rms[:, 16 + qh * 8:16 + qh * 8 + 8],
                                            in_=rms[:, qh * 8:qh * 8 + 8],
                                            func=AF.Ln, bias=1e-6, scale=1.0 / 64),
                                            wait=[(S["dve"], rv)])
                                        ev2 = P.emit("scalar", lambda E, qh=qh: E.activation(
                                            out=rms[:, 32 + qh * 8:32 + qh * 8 + 8],
                                            in_=rms[:, 16 + qh * 8:16 + qh * 8 + 8],
                                            func=AF.Exp, scale=-0.5), inc=S["act"])
                                        gv = None
                                        for j in range(8):
                                            h = qh * 8 + j
                                            wgv = [(S["act"], ev2)] if j == 0 else []
                                            if j == 0 and qh == 0 and ("qrope", par) in bguard:
                                                wgv.append(bguard[("qrope", par)])
                                            gv = P.emit("vector", lambda E, pp=pp, j=j, h=h, par=par: E.scalar_tensor_tensor(
                                                out=qrope[:, par * 512 + h * 32:par * 512 + (h + 1) * 32],
                                                in0=pp[:, j * 64:j * 64 + 32],
                                                scalar=rms[:, 32 + h:32 + h + 1],
                                                in1=qgb[:, h * 64:h * 64 + 32],
                                                op0=ALU.mult, op1=ALU.mult), inc=S["dve"],
                                                wait=wgv)
                                        bguard[("kv", bank)] = (S["dve"], gv)
                                        qdone = (S["dve"], gv)
                                    for hg in range(4 if ("pa_no_q" not in DBG and "pa_no_qt" not in DBG) else 0):
                                        sl = hg % 2
                                        pslot = (pstA, pstB)[sl]
                                        wv = [qdone] if hg == 0 else []
                                        if tg[sl]:
                                            wv.append(tg[sl])
                                        if qst_free is not None and hg == 0 and tcc == 0:
                                            wv.append(qst_free)
                                        tvq = P.emit("tensor", lambda E, pslot=pslot, hg=hg, par=par: E.transpose(
                                            pslot[:],
                                            qrope[:, par * 512 + hg * 128:par * 512 + (hg + 1) * 128],
                                            ident_bf[:]), inc=S["pe"], wait=wv)
                                        evq = P.emit("scalar", lambda E, pslot=pslot, hg=hg, tcc=tcc: E.copy(
                                            out=qst[:, hg * 1024 + tcc * 128:hg * 1024 + (tcc + 1) * 128],
                                            in_=pslot[:]), inc=S["act"],
                                            wait=[(S["pe"], tvq)])
                                        tg[sl] = (S["act"], evq)
                                        bguard[("qrope", par)] = (S["pe"], tvq)
                                    if "pa_no_v" in DBG:
                                        continue
                                    pp = psS[1]
                                    wv = []
                                    if ("kv", 1) in bguard:
                                        wv.append(bguard[("kv", 1)])
                                    mv = P.emit("tensor", lambda E, pp=pp, tcc=tcc: E.matmul(
                                        pp[:], kvT[:, tcc * 128:(tcc + 1) * 128], wv_b[:],
                                        start=True, stop=True), inc=S["pe"], wait=wv)
                                    ti = tc * 8 + tcc
                                    if "pa_v_noevac" in DBG:
                                        vv = P.emit("vector", lambda E, pp=pp: E.tensor_copy(
                                            out=mtp0[:], in_=pp[:]), inc=S["dve"],
                                            wait=[(S["pe"], mv)])
                                        bguard[("kv", 1)] = (S["dve"], vv)
                                        v_done = (S["dve"], vv)
                                        continue
                                    P.emit("vector", lambda E, pp=pp, ti=ti: E.tensor_copy(
                                        out=vt[:, ti * vs_tile:(ti + 1) * vs_tile]
                                        .rearrange("p (h s) -> p h s", s=VS)[:, :, 0:VW],
                                        in_=pp[:].rearrange("p (h s) -> p h s", s=VW)),
                                        wait=[(S["pe"], mv)])
                                    vv = P.emit("vector", lambda E, ti=ti: E.memset(
                                        vt[:, ti * vs_tile:(ti + 1) * vs_tile]
                                        .rearrange("p (h s) -> p h s", s=VS)[:, :, VW:VS], 1.0),
                                        inc=S["dve"])
                                    bguard[("kv", 1)] = (S["dve"], vv)
                                    if "pa_no_k" in DBG:
                                        continue
                                    pp = psS[0]
                                    wv = []
                                    if ("kv", 0) in bguard:
                                        wv.append(bguard[("kv", 0)])
                                    mv = P.emit("tensor", lambda E, pp=pp, tcc=tcc: E.matmul(
                                        pp[:], kvT[:, tcc * 128:(tcc + 1) * 128], wk_b[:],
                                        start=True, stop=True), inc=S["pe"], wait=wv)
                                    if "pa_k_notp" in DBG:
                                        pass
                                    if "pa_k_simple" in DBG:
                                        gvx = P.emit("scalar", lambda E, pp=pp: E.copy(
                                            out=knb[:, 0:512], in_=pp[:]),
                                            inc=S["act"], wait=[(S["pe"], mv)])
                                        bguard[("kv", 0)] = (S["act"], gvx)
                                        continue
                                    sv = P.emit("scalar", lambda E, pp=pp: E.activation(
                                        out=sqf[:], in_=pp[:], func=AF.Square),
                                        inc=S["act"], wait=[(S["pe"], mv)])
                                    rv = P.emit("vector", lambda E: E.tensor_reduce(
                                        out=rms[:, 48:64].rearrange("p (h s) -> p h s", s=1),
                                        in_=sqf[:].rearrange("p (h s) -> p h s", s=32),
                                        axis=AX.X, op=ALU.add), inc=S["dve"],
                                        wait=[(S["act"], sv)])
                                    P.emit("scalar", lambda E: E.activation(
                                        out=rms[:, 64:80], in_=rms[:, 48:64],
                                        func=AF.Ln, bias=1e-6, scale=1.0 / 32),
                                        wait=[(S["dve"], rv)])
                                    ev2 = P.emit("scalar", lambda E: E.activation(
                                        out=rms[:, 80:96], in_=rms[:, 64:80],
                                        func=AF.Exp, scale=-0.5), inc=S["act"])
                                    gv = None
                                    for h in range(16):
                                        if "pa_k_nostt" in DBG:
                                            gv = P.emit("vector", lambda E, pp=pp: E.tensor_copy(
                                                out=mtp0[:], in_=pp[:]), inc=S["dve"])
                                            break
                                        wgv = [(S["act"], ev2)] if h == 0 else []
                                        if h == 0 and ("knb", par) in bguard:
                                            wgv.append(bguard[("knb", par)])
                                        gv = P.emit("vector", lambda E, pp=pp, h=h, par=par: E.scalar_tensor_tensor(
                                            out=knb[:, par * 512 + h * 32:par * 512 + (h + 1) * 32],
                                            in0=pp[:, h * 32:(h + 1) * 32],
                                            scalar=rms[:, 80 + h:80 + h + 1],
                                            in1=kgb[:, h * 32:(h + 1) * 32],
                                            op0=ALU.mult, op1=ALU.mult), inc=S["dve"],
                                            wait=wgv)
                                    bguard[("kv", 0)] = (S["dve"], gv)
                                    for hg in range(4 if "pa_k_notp" not in DBG and "pa_k_nostt" not in DBG else 0):
                                        sl = hg % 2
                                        pslot = (pstA, pstB)[sl]
                                        wv = [(S["dve"], gv)] if hg == 0 else []
                                        if tg[sl]:
                                            wv.append(tg[sl])
                                        tvk = P.emit("tensor", lambda E, pslot=pslot, hg=hg, par=par: E.transpose(
                                            pslot[:],
                                            knb[:, par * 512 + hg * 128:par * 512 + (hg + 1) * 128],
                                            ident_bf[:]), inc=S["pe"], wait=wv)
                                        kdst = (qst[:, hg * 1024 + tcc * 128:hg * 1024 + (tcc + 1) * 128]
                                                if "pa_k_tp2qst" in DBG else
                                                kt[hg][:, tc * 1024 + tcc * 128:tc * 1024 + (tcc + 1) * 128])
                                        evk = P.emit("scalar", lambda E, pslot=pslot, kdst=kdst: E.copy(
                                            out=kdst,
                                            in_=pslot[:]), inc=S["act"],
                                            wait=[(S["pe"], tvk)])
                                        tg[sl] = (S["act"], evk)
                                        bguard[("knb", par)] = (S["pe"], tvk)
                                qsv = P.emit("scalar", lambda E, tc=tc: E.dma_start(
                                    out=qrspill[:, :, tc * 1024:(tc + 1) * 1024].rearrange("i p t -> p i t"),
                                    in_=qst[:].rearrange("p (i t) -> p i t", t=1024)),
                                    inc=S["st"], wait=[(S["act"], S["act"].n)], dma=True)
                                qst_free = (S["st"], qsv)

                def passB(g):
                    with ExitStack() as pb:
                        B_ = pb.enter_context
                        masks = B_(nc.sbuf_tensor(nm(), [128, 2048], F32))
                        negm = B_(nc.sbuf_tensor(nm(), [128, 2048], BF16))
                        qcur = B_(nc.sbuf_tensor(nm(), [128, 1024], BF16))
                        PT = B_(nc.sbuf_tensor(nm(), [128, 1024], BF16))
                        oq = B_(nc.sbuf_tensor(nm(), [128, 2048], BF16))
                        otb = B_(nc.sbuf_tensor(nm(), [128, 512], BF16))
                        xq = B_(nc.sbuf_tensor(nm(), [128, 2048], F32))
                        mcb = B_(nc.sbuf_tensor(nm(), [128, 8], BF16))
                        mtmp = B_(nc.sbuf_tensor(nm(), [128, 512], F32))
                        ptmp = B_(nc.sbuf_tensor(nm(), [128, 1024], F32))
                        psS = [B_(nc.psum_tensor(nm('q'), [128, 512], F32)) for _ in range(2)]
                        psOq = [B_(nc.psum_tensor(nm('q'), [128, 512], F32)) for _ in range(4)]
                        psM = B_(nc.psum_tensor(nm('q'), [128, 128], F32))
                        pstT = B_(nc.psum_tensor(nm('q'), [128, 128], BF16))
                        bguard = {}
                        cmv = P.emit("sync", lambda E: E.dma_start(out=masks[:], in_=cmask_d),
                                     inc=S["lx"])
                        if is0:
                            wov = P.emit("sync", lambda E, g=g: E.dma_start(
                                out=wot_g[:].rearrange("p (i m) -> p i m", m=1024),
                                in_=wcaches["wo_c"].rearrange("i p m -> p i m")[:, g * 4:g * 4 + 4, :]),
                                inc=S["lx"])
                            wo_ready_g = (S["lx"], wov)
                        else:
                            wo_ready_g = wo_ready
                        negm_guard = {}
                        aw_guard = {}
                        oq_free = None

                        def mprep(hl, tsb):
                            if "no_mprep" in DBG:
                                nv = P.emit("vector", lambda E: E.memset(negm[:, 0:1], 0.0), inc=S["dve"])
                                return (S["dve"], nv), (S["dve"], nv)
                            par = hl % 2
                            if is0:
                                b = 64 * (hl % 2)
                                ktile = kt[hl // 2]
                                src = qspill[g * 4 + hl // 2, b:b + HD, tsb * 512:(tsb + 1) * 512]
                                ncol = (hl // 2) * 512
                            else:
                                b = 32 * (hl % 4)
                                ktile = kt[hl // 4]
                                src = qrspill[hl // 4, b:b + HD, tsb * 512:(tsb + 1) * 512]
                                ncol = (hl // 4) * 512
                            wv = []
                            if ("qc", par) in bguard:
                                wv.append(bguard[("qc", par)])
                            qv = P.emit("sync", lambda E, src=src, par=par, b=b: E.dma_start(
                                out=qcur[b:b + HD, par * 512:(par + 1) * 512], in_=src),
                                inc=S["lq"], wait=wv)
                            nmv = None
                            for qt in range(4):
                                qt_abs = tsb * 4 + qt
                                mcol = par * 4 + qt
                                wv = [(S["lq"], qv)]
                                if "psM" in bguard:
                                    wv.append(bguard["psM"])
                                mv = P.emit("tensor", lambda E, b=b, par=par, qt=qt, qt_abs=qt_abs, ktile=ktile: E.matmul(
                                    psM[:], qcur[b:b + HD, par * 512 + qt * 128:par * 512 + (qt + 1) * 128],
                                    ktile[b:b + HD, qt_abs * 128:(qt_abs + 1) * 128],
                                    start=True, stop=True, tile_position=(b, 0)),
                                    inc=S["pe"], wait=wv)
                                rv = P.emit("vector", lambda E, mcol=mcol: E.tensor_reduce(
                                    out=stats[:, mcol:mcol + 1], in_=psM[:], axis=AX.X,
                                    op=ALU.max, negate=True), inc=S["dve"],
                                    wait=[(S["pe"], mv)])
                                wv2 = [(S["dve"], rv)]
                                if ("mst", mcol) in bguard:
                                    wv2.append(bguard[("mst", mcol)])
                                cv_ = P.emit("vector", lambda E, mcol=mcol: E.tensor_copy(
                                    out=mcb[:, mcol:mcol + 1], in_=stats[:, mcol:mcol + 1]),
                                    inc=S["dve"], wait=wv2)
                                mrv = P.emit("tensor", lambda E, mcol=mcol: E.matmul(
                                    psM[0:1, 0:128], mcb[:, mcol:mcol + 1], ident_bf[:],
                                    start=True, stop=True), inc=S["pe"],
                                    wait=[(S["dve"], cv_)])
                                bguard[("mst", mcol)] = (S["pe"], mrv)
                                wv4 = [(S["pe"], mrv)]
                                if hl in negm_guard:
                                    wv4.append(negm_guard[hl])
                                nmv = P.emit("scalar", lambda E, b=b, ncol=ncol, qt=qt: E.copy(
                                    out=negm[b:b + 1, ncol + qt * 128:ncol + (qt + 1) * 128],
                                    in_=psM[0:1, 0:128]), inc=S["act"], wait=wv4)
                                bguard["psM"] = (S["act"], nmv)
                            return (S["act"], nmv), (S["lq"], qv)

                        def sloop(hl, tsb, ready):
                            if "no_scores" in DBG:
                                nv = P.emit("vector", lambda E: E.memset(oq[:, 0:1], 0.0), inc=S["dve"])
                                return (S["dve"], nv)
                            negm_r, q_r = ready
                            par = hl % 2
                            if is0:
                                b = 64 * (hl % 2)
                                ktile = kt[hl // 2]
                                ncol = (hl // 2) * 512
                            else:
                                b = 32 * (hl % 4)
                                ktile = kt[hl // 4]
                                ncol = (hl // 4) * 512
                            pv_stop = {}
                            pv_last = None
                            for sb in range(tsb * 4 + 4):
                                pp = psS[sb % 2]
                                wv = [negm_r, q_r] if sb < 2 else []
                                if ("S", sb % 2) in bguard:
                                    wv.append(bguard[("S", sb % 2)])
                                if sb == 0:
                                    for qt in range(4):
                                        if ("O", qt) in bguard:
                                            wv.append(bguard[("O", qt)])
                                P.emit("tensor", lambda E, pp=pp, b=b, sb=sb, ktile=ktile, par=par: E.matmul(
                                    pp[:], ktile[b:b + HD, sb * 128:(sb + 1) * 128],
                                    qcur[b:b + HD, par * 512:(par + 1) * 512],
                                    start=True, stop=False, tile_position=(b, 0)), wait=wv)
                                mv = P.emit("tensor", lambda E, pp=pp, b=b, ncol=ncol: E.matmul(
                                    pp[:], ones_t[b:b + 1, 0:128],
                                    negm[b:b + 1, ncol:ncol + 512],
                                    start=False, stop=True, tile_position=(b, 0)), inc=S["pe"])
                                ready_ = (S["pe"], mv)
                                exp_src = pp
                                if sb >= tsb * 4:
                                    k_ = sb - tsb * 4
                                    wvm = [ready_, (S["lx"], cmv)]
                                    if "mtmp" in bguard:
                                        wvm.append(bguard["mtmp"])
                                    dv = P.emit("vector", lambda E, pp=pp, k_=k_: E.tensor_tensor(
                                        out=mtmp[:], in0=pp[:], in1=masks[:, k_ * 512:(k_ + 1) * 512],
                                        op=ALU.add), inc=S["dve"], wait=wvm)
                                    ready_ = (S["dve"], dv)
                                    exp_src = mtmp
                                wv2 = [ready_]
                                if ("PT", sb % 2) in bguard:
                                    wv2.append(bguard[("PT", sb % 2)])
                                ev = P.emit("scalar", lambda E, exp_src=exp_src, sb=sb: E.activation(
                                    out=PT[:, (sb % 2) * 512:(sb % 2 + 1) * 512],
                                    in_=exp_src[:] if exp_src is mtmp else exp_src[:],
                                    func=AF.Exp, scale=0.125), inc=S["act"], wait=wv2)
                                bguard[("S", sb % 2)] = (S["act"], ev)
                                if exp_src is mtmp:
                                    bguard["mtmp"] = (S["act"], ev)
                                for qt in range(max(0, sb - tsb * 4), 4):
                                    if "no_pv" in DBG:
                                        pv_stop[qt] = bguard[("S", sb % 2)]
                                        pv_last = mv
                                        continue
                                    qt_abs = tsb * 4 + qt
                                    wv3 = [(S["act"], ev)]
                                    if sb == 0 and ("O", qt) in bguard:
                                        wv3.append(bguard[("O", qt)])
                                    pv_last = P.emit("tensor", lambda E, qt=qt, sb=sb, hl=hl, qt_abs=qt_abs: E.matmul(
                                        psOq[qt][:, 0:VS],
                                        PT[:, (sb % 2) * 512 + qt * 128:(sb % 2) * 512 + (qt + 1) * 128],
                                        vt[:, sb * vs_tile + hl * VS:sb * vs_tile + (hl + 1) * VS],
                                        start=(sb == 0), stop=(sb == qt_abs)), inc=S["pe"],
                                        wait=wv3 if qt == max(0, sb - tsb * 4) else [])
                                    if sb == qt_abs:
                                        pv_stop[qt] = (S["pe"], pv_last)
                                bguard[("PT", sb % 2)] = (S["pe"], pv_last)
                            ovv = None
                            if "no_pv" in DBG:
                                nv = P.emit("vector", lambda E: E.memset(oq[:, 0:1], 0.0), inc=S["dve"])
                                bguard[("qc", par)] = (S["pe"], pv_last)
                                negm_guard[hl] = (S["pe"], pv_last)
                                return (S["dve"], nv)
                            for qt in range(4):
                                wv4 = [pv_stop[qt]]
                                if oq_free is not None:
                                    wv4.append(oq_free)
                                P.emit("vector", lambda E, qt=qt: E.tensor_scalar(
                                    out=stats[:, 8 + qt:9 + qt],
                                    in0=psOq[qt][:, VW:VS],
                                    scalar1=1e-12, scalar2=None, op0=ALU.max), wait=wv4)
                                P.emit("vector", lambda E, qt=qt: E.reciprocal(
                                    out=stats[:, 12 + qt:13 + qt], in_=stats[:, 8 + qt:9 + qt]))
                                ovv = P.emit("vector", lambda E, qt=qt, hl=hl: E.tensor_scalar(
                                    out=oq[:, qt * GW + hl * VW:qt * GW + (hl + 1) * VW],
                                    in0=psOq[qt][:, 0:VW],
                                    scalar1=stats[:, 12 + qt:13 + qt], scalar2=None,
                                    op0=ALU.mult), inc=S["dve"])
                                bguard[("O", qt)] = (S["dve"], ovv)
                            bguard[("qc", par)] = (S["pe"], pv_last)
                            negm_guard[hl] = (S["pe"], pv_last)
                            return (S["dve"], ovv)

                        for tsb in range(nTSB):
                            if 'A' in hammer:
                                barrier()
                            pend = {}
                            pend[0] = mprep(0, tsb)
                            if 'B' in hammer:
                                barrier()
                            if HPG > 1:
                                pend[1] = mprep(1, tsb)
                                if 'C' in hammer:
                                    barrier()
                            o_done = []
                            for i in range(HPG):
                                o_done.append(sloop(i, tsb, pend[i]))
                                if 'D' in hammer:
                                    barrier()
                                if i + 2 < HPG:
                                    pend[i + 2] = mprep(i + 2, tsb)
                                    if 'E' in hammer:
                                        barrier()
                            if "no_outproj" in DBG:
                                continue
                            tp_last = None
                            tv_last = None
                            for qt in range(4):
                                if 'F' in hammer:
                                    barrier()
                                qt_abs = tsb * 4 + qt
                                xpar = qt % 2
                                xsrc = x_in if g == 0 else x_out
                                wv = o_done[-2:]
                                if ("xq", xpar) in bguard:
                                    wv.append(bguard[("xq", xpar)])
                                lv = P.emit("sync", lambda E, qt_abs=qt_abs, xpar=xpar, xsrc=xsrc: E.dma_start(
                                    out=xq[:, xpar * 1024:(xpar + 1) * 1024],
                                    in_=xsrc[qt_abs * 128:(qt_abs + 1) * 128, :]),
                                    inc=S["lx"],
                                    wait=wv + ([(S["st"], S["st"].n)] if g == 1 else []))
                                for oc in range(4):
                                    wv2 = list(o_done[-2:]) if oc == 0 else []
                                    if "ot" in bguard:
                                        wv2.append(bguard["ot"])
                                    if ("otb",) in bguard and oc == 0:
                                        wv2.append(bguard[("otb",)])
                                    tv = P.emit("tensor", lambda E, oc=oc, qt=qt: E.transpose(
                                        pstT[:], oq[:, qt * GW + oc * 128:qt * GW + (oc + 1) * 128],
                                        ident_bf[:]), inc=S["pe"], wait=wv2)
                                    evv = P.emit("scalar", lambda E, oc=oc: E.copy(
                                        out=otb[:, oc * 128:(oc + 1) * 128], in_=pstT[:]),
                                        inc=S["act"], wait=[(S["pe"], tv)])
                                    bguard["ot"] = (S["act"], evv)
                                    tp_last = (S["act"], evv)
                                    tv_last = tv
                                if "op_tponly" in DBG:
                                    bguard[("otb",)] = (S["pe"], tv_last)
                                    sv = P.emit("scalar", lambda E, qt_abs=qt_abs, xpar=xpar: E.dma_start(
                                        out=x_out[qt_abs * 128:(qt_abs + 1) * 128, :],
                                        in_=xq[:, xpar * 1024:(xpar + 1) * 1024]),
                                        inc=S["st"], wait=[(S["lx"], lv), tp_last], dma=True)
                                    bguard[("xq", xpar)] = (S["st"], sv)
                                    continue
                                for half in range(2):
                                    bank = half
                                    pp = psS[bank]
                                    wv3 = [tp_last, wo_ready_g]
                                    if ("S", bank) in bguard:
                                        wv3.append(bguard[("S", bank)])
                                    mv = None
                                    for oc in range(4):
                                        mv = P.emit("tensor", lambda E, pp=pp, oc=oc, half=half: E.matmul(
                                            pp[:], otb[:, oc * 128:(oc + 1) * 128],
                                            wot_g[:, oc * 1024 + half * 512:oc * 1024 + (half + 1) * 512],
                                            start=(oc == 0), stop=(oc == 3)), inc=S["pe"],
                                            wait=wv3 if oc == 0 else [])
                                    if "op_nomm" in DBG:
                                        mv = P.emit("tensor", lambda E: E.nop(), inc=S["pe"],
                                                    wait=wv3)
                                    if "op_noadd" in DBG or ("op_add0" in DBG and half == 1) or ("op_add1" in DBG and half == 0):
                                        av = P.emit("vector", lambda E, xpar=xpar: E.memset(
                                            xq[:, xpar * 1024:xpar * 1024 + 1], 0.0), inc=S["dve"],
                                            wait=[(S["pe"], mv), (S["lx"], lv)])
                                    else:
                                        pcv = P.emit("scalar", lambda E, pp=pp, half=half: E.copy(
                                            out=ptmp[:, half * 512:(half + 1) * 512], in_=pp[:]),
                                            inc=S["act"], wait=[(S["pe"], mv)] + ([aw_guard[half]] if half in aw_guard else []))
                                        av = P.emit("vector", lambda E, xpar=xpar, half=half: E.tensor_tensor(
                                            out=xq[:, xpar * 1024 + half * 512:xpar * 1024 + (half + 1) * 512],
                                            in0=xq[:, xpar * 1024 + half * 512:xpar * 1024 + (half + 1) * 512],
                                            in1=ptmp[:, half * 512:(half + 1) * 512], op=ALU.add),
                                            inc=S["dve"],
                                            wait=[(S["act"], pcv), (S["lx"], lv)])
                                        aw_guard[half] = (S["dve"], av)
                                    bguard[("S", bank)] = (S["dve"], av)
                                bguard[("otb",)] = (S["pe"], mv)
                                sv = P.emit("scalar", lambda E, qt_abs=qt_abs, xpar=xpar: E.dma_start(
                                    out=x_out[qt_abs * 128:(qt_abs + 1) * 128, :],
                                    in_=xq[:, xpar * 1024:(xpar + 1) * 1024]),
                                    inc=S["st"], wait=[bguard[("S", 1)]], dma=True)
                                bguard[("xq", xpar)] = (S["st"], sv)
                            if 'G' in hammer:
                                barrier()
                            oq_free = (S["pe"], tv_last)

                for g in range(NG):
                    passA(g)
                    barrier()
                    if "passA_only" not in DBG:
                        passB(g)
                    barrier()

        cur = x0
        pool = [xa, xb]
        for i, sname in enumerate(seq):
            dst = out_d if i == len(seq) - 1 else pool[i % 2]
            if sname == "f0":
                emit_ffn("l0", cur, dst)
            elif sname == "f1":
                emit_ffn("l1", cur, dst)
            elif sname == "a0":
                emit_attn("l0", cur, dst)
            elif sname == "a1":
                emit_attn("l1", cur, dst)
            barrier()
            cur = dst

        P.waits("scalar", (S["st"], S["st"].n))
        P.waits("sync", (S["st"], S["st"].n))

        with nc.Block() as block:
            @block.sync
            def _(E):
                for f in P.q["sync"]:
                    f(E)

            @block.tensor
            def _(E):
                for f in P.q["tensor"]:
                    f(E)

            @block.vector
            def _(E):
                for f in P.q["vector"]:
                    f(E)

            @block.scalar
            def _(E):
                for f in P.q["scalar"]:
                    f(E)

            @block.gpsimd
            def _(E):
                for f in P.q["gpsimd"]:
                    f(E)
    return nc


def make_consts():
    ident_bf = np.eye(128, dtype=ml_dtypes.bfloat16)
    ident_f = np.eye(128, dtype=np.float32)
    cmask = np.zeros((128, 2048), np.float32)
    for k in range(4):
        p = np.arange(128)[:, None]
        j = np.arange(512)[None, :]
        cmask[:, k * 512:(k + 1) * 512] = np.where(k * 128 + p > j, NEG, 0.0).astype(np.float32)
    return {"ident_bf": ident_bf, "ident_f": ident_f, "cmask": cmask}


def kernel(**inputs):
    T = 4096
    nc = build(T=T, hammer="ABCDEG")
    consts = make_consts()
    x = np.ascontiguousarray(inputs["L0_attn_x"], dtype=np.float32)
    base = {k: np.ascontiguousarray(v) for k, v in inputs.items() if k != "L0_attn_x"}
    base.update(consts)
    in_maps = []
    for c in range(8):
        m = dict(base)
        m["x0"] = x[c]
        in_maps.append(m)
    res = run_bass_kernel_spmd(nc, in_maps, core_ids=list(range(8)))
    return np.stack([np.asarray(res.results[c]["out"], dtype=np.float32) for c in range(8)])


# version 05 (final)
# speedup vs baseline: 1.0306x; 1.0306x over previous
import numpy as np
import ml_dtypes
from contextlib import ExitStack

import concourse.bass as bass
import concourse.mybir as mybir
from concourse.bass_utils import run_bass_kernel_spmd

F32 = mybir.dt.float32
BF16 = mybir.dt.bfloat16
AF = mybir.ActivationFunctionType
ALU = mybir.AluOpType
AX = mybir.AxisListType

ENGINES = ("sync", "tensor", "vector", "scalar", "gpsimd")
DBG = set()
NEG = -1.0e30


class Sem:
    __slots__ = ("h", "n", "step")

    def __init__(self, h, step):
        self.h = h
        self.n = 0
        self.step = step

    def bump(self):
        self.n += self.step
        return self.n


class Plan:
    AUTO = ("vector", "scalar", "gpsimd")

    def __init__(self):
        self.q = {e: [] for e in ENGINES}
        self.esem = {}

    def emit(self, eng, fn, inc=None, wait=(), dma=False):
        w = list((s, v) for (s, v) in wait if s is not None and v)
        auto = eng in self.AUTO and not dma
        selfwait = None
        own = self.esem.get(eng)
        if auto and own is not None:
            if own.n:
                selfwait = (own, own.n)
            if inc is None:
                inc = own
        if selfwait is not None:
            w.append(selfwait)
        w = tuple(w)
        val = inc.bump() if inc is not None else None
        extra = None
        if auto and own is not None and inc is not own:
            extra = (own, own.bump())

        def run(E, fn=fn, w=w, inc=inc, extra=extra):
            for s, v in w:
                E.wait_ge(s.h, v)
            ins = fn(E)
            if inc is not None:
                ins = ins.then_inc(inc.h, inc.step)
            if extra is not None:
                ins.then_inc(extra[0].h, extra[0].step)
        self.q[eng].append(run)
        return val

    def waits(self, eng, *wait):
        w = tuple((s, v) for (s, v) in wait if s is not None and v)
        if not w:
            return

        def run(E, w=w):
            for s, v in w:
                E.wait_ge(s.h, v)
        self.q[eng].append(run)


def build(T=4096, seq=None, hammer=''):
    D = 1024
    nT8 = T // 1024
    nTSB = T // 512
    assert T % 1024 == 0
    if seq is None:
        seq = ["f0", "a0", "f0", "f1", "a1", "f1"]

    nc = bass.Bass("TRN2", target_bir_lowering=False, detect_race_conditions=False)

    x0 = nc.dram_tensor("x0", [T, D], F32, kind="ExternalInput").ap()
    wd = {}
    for name, shape in [
        ("L0_attn_Wq", [D, D]), ("L0_attn_Wk", [D, D]), ("L0_attn_Wv", [D, D]),
        ("L0_attn_Wo", [D, D]), ("L0_ffn_router", [D, 8]),
        ("L0_ffn_experts_gate", [8, D, 512]), ("L0_ffn_experts_up", [8, D, 512]),
        ("L0_ffn_experts_down", [8, 512, D]), ("L0_ffn_shared_gate", [D, D]),
        ("L0_ffn_shared_up", [D, D]), ("L0_ffn_shared_down", [D, D]),
        ("L1_attn_Wq", [D, D]), ("L1_attn_Wkv", [D, 128]), ("L1_attn_Wk", [128, 512]),
        ("L1_attn_Wv", [128, 512]), ("L1_attn_Wo", [D, D]),
        ("L1_attn_q_g", [64]), ("L1_attn_k_g", [32]),
        ("L1_ffn_router_w", [D, 14]), ("L1_ffn_w1", [14, D, 2048]),
        ("L1_ffn_b1", [14, 2048]), ("L1_ffn_w2", [14, 2048, D]),
        ("L1_ffn_b2", [14, D]),
    ]:
        wd[name] = nc.dram_tensor(name, shape, F32, kind="ExternalInput").ap()
    ident_bf_d = nc.dram_tensor("ident_bf", [128, 128], BF16, kind="ExternalInput").ap()
    ident_f_d = nc.dram_tensor("ident_f", [128, 128], F32, kind="ExternalInput").ap()
    cmask_d = nc.dram_tensor("cmask", [128, 2048], F32, kind="ExternalInput").ap()

    out_d = nc.dram_tensor("out", [T, D], F32, kind="ExternalOutput").ap()
    xa = nc.dram_tensor("xa", [T, D], F32, kind="Internal").ap()
    xb = nc.dram_tensor("xb", [T, D], F32, kind="Internal").ap()
    qspill = nc.dram_tensor("qspill", [8, 128, T], BF16, kind="Internal").ap()
    qrspill = nc.dram_tensor("qrspill", [4, 128, T], BF16, kind="Internal").ap()
    wcaches = {w: nc.dram_tensor(w, [8, 128, D], BF16, kind="Internal").ap()
               for w in ("wq_c", "wk_c", "wv_c", "wo_c")}
    c_l1w1 = nc.dram_tensor("c_l1w1", [14, 8, 128, 2048], BF16, kind="Internal").ap()
    c_l1w2 = nc.dram_tensor("c_l1w2", [14, 16, 128, 1024], BF16, kind="Internal").ap()
    c_l0w1 = nc.dram_tensor("c_l0w1", [8, 8, 128, 1024], BF16, kind="Internal").ap()
    c_l0w2 = nc.dram_tensor("c_l0w2", [8, 4, 128, 1024], BF16, kind="Internal").ap()
    c_sh1 = nc.dram_tensor("c_sh1", [8, 128, 2048], BF16, kind="Internal").ap()
    c_sh2 = nc.dram_tensor("c_sh2", [8, 128, 1024], BF16, kind="Internal").ap()

    P = Plan()
    _nctr = [0]

    def nm(p="t"):
        _nctr[0] += 1
        return f"_{p}{_nctr[0]}"

    with ExitStack() as top:
        TE = top.enter_context
        S = {
            "lx": Sem(TE(nc.semaphore("s_lx")), 16),
            "lw": Sem(TE(nc.semaphore("s_lw")), 16),
            "lw2": Sem(TE(nc.semaphore("s_lw2")), 16),
            "lq": Sem(TE(nc.semaphore("s_lq")), 16),
            "st": Sem(TE(nc.semaphore("s_st")), 16),
            "pe": Sem(TE(nc.semaphore("s_pe")), 1),
            "act": Sem(TE(nc.semaphore("s_act")), 1),
            "dve": Sem(TE(nc.semaphore("s_dve")), 1),
            "pool": Sem(TE(nc.semaphore("s_pool")), 1),
        }
        ENG_SEM = {"tensor": S["pe"], "vector": S["dve"], "scalar": S["act"],
                   "gpsimd": S["pool"]}
        P.esem = {"vector": S["dve"], "scalar": S["act"], "gpsimd": S["pool"]}

        def barrier():
            vals = []
            for e in ("tensor", "vector", "scalar", "gpsimd"):
                v = P.emit(e, lambda E: E.nop(), inc=ENG_SEM[e])
                vals.append((e, v))
            for e in ENGINES:
                P.waits(e, *[(ENG_SEM[o], v) for o, v in vals if o != e])
            P.waits("sync", (S["st"], S["st"].n))

        ident_bf = TE(nc.sbuf_tensor("ident_bf_s", [128, 128], BF16))
        ident_f = TE(nc.sbuf_tensor("ident_f_s", [128, 128], F32))
        ones_t = TE(nc.sbuf_tensor("ones_t", [128, 128], BF16))
        b1T = TE(nc.sbuf_tensor("b1T", [128, 224], F32))
        b2p = TE(nc.sbuf_tensor("b2p", [128, 4096], BF16))
        qgb = TE(nc.sbuf_tensor("qgb", [128, 1024], F32))
        kgb = TE(nc.sbuf_tensor("kgb", [128, 512], F32))
        cst = TE(nc.sbuf_tensor("cst", [128, 2], F32))
        nc.const_aps.register(F32, 1e-5, cst[:, 0:1])
        nc.const_aps.register(F32, 1e-6, cst[:, 1:2])

        c0 = P.emit("sync", lambda E: E.dma_start(out=ident_bf[:], in_=ident_bf_d), inc=S["lx"])
        c1 = P.emit("sync", lambda E: E.dma_start(out=ident_f[:], in_=ident_f_d), inc=S["lx"])
        P.emit("vector", lambda E: E.memset(ones_t[:], 1.0), inc=S["dve"])
        P.emit("vector", lambda E: E.memset(cst[:, 0:1], 1e-5))
        P.emit("vector", lambda E: E.memset(cst[:, 1:2], 1e-6), inc=S["dve"])
        P.waits("tensor", (S["lx"], c1), (S["dve"], S["dve"].n))
        P.waits("gpsimd", (S["lx"], c1))
        P.waits("scalar", (S["lx"], c1))

        with ExitStack() as prep:
            PE_ = prep.enter_context
            stg = PE_(nc.sbuf_tensor("p_stg", [128, 4096], F32))
            psA = PE_(nc.psum_tensor("p_psA", [128, 512], F32))
            psB = PE_(nc.psum_tensor("p_psB", [128, 512], F32))
            ptA = PE_(nc.psum_tensor("p_ptA", [128, 128], BF16))
            ptB = PE_(nc.psum_tensor("p_ptB", [128, 128], BF16))
            pt14 = PE_(nc.psum_tensor("p_pt14", [128, 28], F32))

            if "f1" in seq:
                b1s = PE_(nc.sbuf_tensor("p_b1s", [14, 2048], F32))
                v = P.emit("sync", lambda E: E.dma_start(out=b1s[:], in_=wd["L1_ffn_b1"]), inc=S["lw"])
                bsv = None
                for e in range(14):
                    bsv = P.emit("sync", lambda E, e=e: E.dma_start(
                        out=stg[32 * (e % 4):32 * (e % 4) + 1, (e // 4) * 1024:(e // 4) * 1024 + 1024],
                        in_=wd["L1_ffn_b2"][e:e + 1, :]), inc=S["lw"])
                P.emit("gpsimd", lambda E: E.memset(b2p[:], 0.0), inc=S["pool"])
                P.emit("gpsimd", lambda E, e=e: E.tensor_copy(
                    out=b2p[0:1, 0:4096], in_=stg[0:1, 0:4096]), wait=[(S["lw"], bsv)])
                P.emit("gpsimd", lambda E: E.tensor_copy(
                    out=b2p[32:33, 0:4096], in_=stg[32:33, 0:4096]))
                P.emit("gpsimd", lambda E: E.tensor_copy(
                    out=b2p[64:65, 0:3072], in_=stg[64:65, 0:3072]))
                b2v = P.emit("gpsimd", lambda E: E.tensor_copy(
                    out=b2p[96:97, 0:3072], in_=stg[96:97, 0:3072]), inc=S["pool"])
                last_evac = None
                for fc in range(16):
                    slot = pt14[:, (fc % 2) * 14:(fc % 2) * 14 + 14]
                    wv = [(S["lw"], v), (S["lx"], c1)]
                    if last_evac and fc >= 2:
                        wv.append(last_evac)
                    tv = P.emit("tensor", lambda E, s=slot, fc=fc: E.transpose(
                        s, b1s[:, fc * 128:(fc + 1) * 128], ident_f[0:14, 0:14]),
                        inc=S["pe"], wait=wv)
                    ev = P.emit("scalar", lambda E, s=slot, fc=fc: E.copy(
                        out=b1T[:, fc * 14:(fc + 1) * 14], in_=s), inc=S["act"],
                        wait=[(S["pe"], tv)])
                    last_evac = (S["act"], ev)
            if "a1" in seq:
                qg1 = PE_(nc.sbuf_tensor("p_qg1", [1, 64], F32))
                kg1 = PE_(nc.sbuf_tensor("p_kg1", [1, 32], F32))
                qg1b = PE_(nc.sbuf_tensor("p_qg1b", [1, 64], BF16))
                kg1b = PE_(nc.sbuf_tensor("p_kg1b", [1, 32], BF16))
                v = P.emit("sync", lambda E: E.dma_start(out=qg1[:], in_=wd["L1_attn_q_g"][None, :]), inc=S["lw"])
                v2 = P.emit("sync", lambda E: E.dma_start(out=kg1[:], in_=wd["L1_attn_k_g"][None, :]), inc=S["lw"])
                cv = P.emit("gpsimd", lambda E: E.tensor_copy(out=qg1b[:], in_=qg1[:]),
                            inc=S["pool"], wait=[(S["lw"], v)])
                cv2 = P.emit("gpsimd", lambda E: E.tensor_copy(out=kg1b[:], in_=kg1[:]),
                             inc=S["pool"], wait=[(S["lw"], v2)])
                for half in range(2):
                    ps = (psA, psB)[half]
                    mv = None
                    for j in range(8):
                        mv = P.emit("tensor", lambda E, ps=ps, j=j: E.matmul(
                            ps[:, j * 64:(j + 1) * 64], ones_t[0:1, 0:128], qg1b[:],
                            start=True, stop=True), inc=S["pe"],
                            wait=[(S["pool"], cv)] if j == 0 else [])
                    P.emit("scalar", lambda E, ps=ps, half=half: E.copy(
                        out=qgb[:, half * 512:(half + 1) * 512], in_=ps[:]),
                        inc=S["act"], wait=[(S["pe"], mv)])
                mv = None
                wv = [(S["pool"], cv2), (S["act"], S["act"].n)]
                for h in range(16):
                    mv = P.emit("tensor", lambda E, h=h: E.matmul(
                        psA[:, h * 32:(h + 1) * 32], ones_t[0:1, 0:128], kg1b[:],
                        start=True, stop=True), inc=S["pe"],
                        wait=wv if h == 0 else [])
                P.emit("scalar", lambda E: E.copy(out=kgb[:], in_=psA[:]),
                       inc=S["act"], wait=[(S["pe"], mv)])
            if "a0" in seq:
                wtmp = PE_(nc.sbuf_tensor("p_wtmp", [128, 8192], BF16))
                wTst = PE_(nc.sbuf_tensor("p_wTst", [128, 8192], BF16))
                stg_free = [None, None]
                wtst_store = None
                wtmp_used = None
                for wname, cache in [("L0_attn_Wq", "wq_c"), ("L0_attn_Wk", "wk_c"),
                                     ("L0_attn_Wv", "wv_c"), ("L0_attn_Wo", "wo_c")]:
                    w_ap = wd[wname]
                    cast_vals = []
                    for j in range(4):
                        par = j % 2
                        dv = P.emit("sync", lambda E, w_ap=w_ap, j=j, par=par: E.dma_start(
                            out=stg[:, par * 2048:(par + 1) * 2048].rearrange("p (a q) -> p a q", q=1024),
                            in_=w_ap.rearrange("(a p) m -> p a m", p=128)[:, 2 * j:2 * j + 2, :]),
                            inc=S["lw"], wait=[stg_free[par]] if stg_free[par] else [])
                        cw = [(S["lw"], dv)]
                        if wtmp_used is not None and j == 0:
                            cw.append(wtmp_used)
                        cvv = P.emit("gpsimd", lambda E, j=j, par=par: E.tensor_copy(
                            out=wtmp[:, j * 2048:(j + 1) * 2048],
                            in_=stg[:, par * 2048:(par + 1) * 2048]),
                            inc=S["pool"], wait=cw)
                        stg_free[par] = (S["pool"], cvv)
                        cast_vals.append(cvv)
                    tguard = [None, None]
                    tv_last = None
                    for mc in range(8):
                        for ncc in range(8):
                            k = mc * 8 + ncc
                            slot = (ptA, ptB)[k % 2]
                            w_ = [(S["pool"], cast_vals[ncc // 2]), (S["lx"], c0)]
                            if tguard[k % 2]:
                                w_.append(tguard[k % 2])
                            if wtst_store is not None and k == 0:
                                w_.append(wtst_store)
                            tv = P.emit("tensor", lambda E, slot=slot, mc=mc, ncc=ncc: E.transpose(
                                slot[:], wtmp[:, ncc * 1024 + mc * 128:ncc * 1024 + (mc + 1) * 128],
                                ident_bf[:]), inc=S["pe"], wait=w_)
                            ev = P.emit("scalar", lambda E, slot=slot, mc=mc, ncc=ncc: E.copy(
                                out=wTst[:, mc * 1024 + ncc * 128:mc * 1024 + (ncc + 1) * 128],
                                in_=slot[:]), inc=S["act"], wait=[(S["pe"], tv)])
                            tguard[k % 2] = (S["act"], ev)
                            tv_last = tv
                    wtmp_used = (S["pe"], tv_last)
                    sv = P.emit("scalar", lambda E, cache=cache: E.dma_start(
                        out=wcaches[cache].rearrange("i p m -> p i m"),
                        in_=wTst[:].rearrange("p (i m) -> p i m", m=1024)), inc=S["st"],
                        wait=[(S["act"], S["act"].n)], dma=True)
                    wtst_store = (S["st"], sv)
        barrier()
        with ExitStack() as prep2:
            PE2 = prep2.enter_context
            stgc = PE2(nc.sbuf_tensor("c_stg", [128, 4096], F32))
            cbf = PE2(nc.sbuf_tensor("c_cbf", [128, 4096], BF16))
            jobs = []
            if "f1" in seq:
                for e in range(14):
                    w1s = wd["L1_ffn_w1"][e].rearrange("(a p) f -> p a f", p=128)
                    for dc in range(8):
                        jobs.append(([(w1s[:, dc, :], 0, 2048)], c_l1w1[e, dc], 2048))
                    w2s = wd["L1_ffn_w2"][e].rearrange("(a p) q -> p a q", p=128)
                    for j in range(8):
                        jobs.append(([(w2s[:, 2 * j:2 * j + 2, :], 0, (2, 1024))],
                                     c_l1w2[e, 2 * j:2 * j + 2].rearrange("a p q -> p a q"), 2048))
            if "f0" in seq:
                for e in range(8):
                    g_ = wd["L0_ffn_experts_gate"][e].rearrange("(a p) f -> p a f", p=128)
                    u_ = wd["L0_ffn_experts_up"][e].rearrange("(a p) f -> p a f", p=128)
                    d_ = wd["L0_ffn_experts_down"][e].rearrange("(a p) q -> p a q", p=128)
                    for dc in range(8):
                        jobs.append(([(g_[:, dc, :], 0, 512), (u_[:, dc, :], 512, 512)],
                                     c_l0w1[e, dc], 1024))
                    for j in range(2):
                        jobs.append(([(d_[:, 2 * j:2 * j + 2, :], 0, (2, 1024))],
                                     c_l0w2[e, 2 * j:2 * j + 2].rearrange("a p q -> p a q"), 2048))
                g_ = wd["L0_ffn_shared_gate"].rearrange("(a p) f -> p a f", p=128)
                u_ = wd["L0_ffn_shared_up"].rearrange("(a p) f -> p a f", p=128)
                d_ = wd["L0_ffn_shared_down"].rearrange("(a p) q -> p a q", p=128)
                for dc in range(8):
                    jobs.append(([(g_[:, dc, :], 0, 1024), (u_[:, dc, :], 1024, 1024)],
                                 c_sh1[dc], 2048))
                for j in range(4):
                    jobs.append(([(d_[:, 2 * j:2 * j + 2, :], 0, (2, 1024))],
                                 c_sh2[2 * j:2 * j + 2].rearrange("a p q -> p a q"), 2048))
            stgc_free = [None, None]
            cbf_free = [None, None]
            for ji, (srcs, dst, width) in enumerate(jobs):
                par = ji % 2
                dv = None
                for k, (src, off, w_) in enumerate(srcs):
                    sgv = stgc[:, par * 2048 + off:par * 2048 + off +
                               (w_ if isinstance(w_, int) else w_[0] * w_[1])]
                    if not isinstance(w_, int):
                        sgv = sgv.rearrange("p (a q) -> p a q", q=w_[1])
                    wv = [stgc_free[par]] if (stgc_free[par] and k == 0) else []
                    dv = P.emit("sync", lambda E, src=src, sgv=sgv: E.dma_start(
                        out=sgv, in_=src), inc=S["lw"], wait=wv)
                ceng = "vector" if ji % 2 == 0 else "scalar"
                if ceng == "vector":
                    cv = P.emit("vector", lambda E, par=par, width=width: E.tensor_copy(
                        out=cbf[:, par * 2048:par * 2048 + width],
                        in_=stgc[:, par * 2048:par * 2048 + width]),
                        inc=S["dve"],
                        wait=[(S["lw"], dv)] + ([cbf_free[par]] if cbf_free[par] else []))
                    cs = (S["dve"], cv)
                else:
                    cv = P.emit("scalar", lambda E, par=par, width=width: E.copy(
                        out=cbf[:, par * 2048:par * 2048 + width],
                        in_=stgc[:, par * 2048:par * 2048 + width]),
                        inc=S["act"],
                        wait=[(S["lw"], dv)] + ([cbf_free[par]] if cbf_free[par] else []))
                    cs = (S["act"], cv)
                stgc_free[par] = cs
                src_out = cbf[:, par * 2048:par * 2048 + width]
                if len(dst.shape) == 3:
                    src_out = src_out.rearrange("p (a q) -> p a q", q=dst.shape[2])
                ov = P.emit("sync", lambda E, dst=dst, src_out=src_out: E.dma_start(
                    out=dst, in_=src_out), inc=S["st"], wait=[cs])
                cbf_free[par] = (S["st"], ov)
        barrier()

        def make_ln(bufs):
            state = {"sq_done": [None, None], "lny_free": [None, None],
                     "tguard": bufs.get("tguard", [None, None]), "tctr": 0}

            def ln_tc(x_ready, yT_war=None):
                xbuf, lnx, lnsq, lny, stats, yT = (bufs["xbuf"], bufs["lnx"],
                                                   bufs["lnsq"], bufs["lny"],
                                                   bufs["stats"], bufs["yT"])
                pts = bufs["pt"]
                ytv = []
                last_x_read = None
                for tcc in range(8):
                    par = tcc % 2
                    o = par * 16
                    st_ = stats
                    xt = xbuf[:, tcc * 1024:(tcc + 1) * 1024]
                    P.emit("vector", lambda E, xt=xt, o=o, st_=st_: E.reduce_sum(
                        out=st_[:, o:o + 1], in_=xt, axis=AX.X),
                        wait=[x_ready] if tcc == 0 else [])
                    P.emit("vector", lambda E, o=o, st_=st_: E.tensor_scalar_mul(
                        st_[:, o + 1:o + 2], st_[:, o:o + 1], 1.0 / 1024))
                    wv = [state["sq_done"][par]] if state["sq_done"][par] else []
                    dv = P.emit("vector", lambda E, xt=xt, o=o, par=par, st_=st_: E.tensor_scalar(
                        out=lnx[:, par * 1024:(par + 1) * 1024], in0=xt,
                        scalar1=st_[:, o + 1:o + 2], scalar2=None, op0=ALU.subtract),
                        inc=S["dve"], wait=wv)
                    last_x_read = (S["dve"], dv)
                    sv = P.emit("scalar", lambda E, o=o, par=par, st_=st_: E.activation(
                        out=lnsq[:], in_=lnx[:, par * 1024:(par + 1) * 1024],
                        func=AF.Square, accum_out=st_[:, o + 2:o + 3]),
                        inc=S["act"], wait=[(S["dve"], dv)])
                    state["sq_done"][par] = (S["act"], sv)
                    P.emit("scalar", lambda E, o=o, st_=st_: E.activation(
                        out=st_[:, o + 3:o + 4], in_=st_[:, o + 2:o + 3],
                        func=AF.Ln, bias=1e-5, scale=1.0 / 1024))
                    ev = P.emit("scalar", lambda E, o=o, st_=st_: E.activation(
                        out=st_[:, o + 4:o + 5], in_=st_[:, o + 3:o + 4],
                        func=AF.Exp, scale=-0.5), inc=S["act"])
                    wv = [(S["act"], ev)]
                    if state["lny_free"][par]:
                        wv.append(state["lny_free"][par])
                    yv = P.emit("vector", lambda E, o=o, par=par, st_=st_: E.tensor_scalar(
                        out=lny[:, par * 1024:(par + 1) * 1024],
                        in0=lnx[:, par * 1024:(par + 1) * 1024],
                        scalar1=st_[:, o + 4:o + 5], scalar2=None, op0=ALU.mult),
                        inc=S["dve"], wait=wv)
                    last_ev = None
                    tv = None
                    for dc in range(8):
                        sl = state["tctr"] % 2
                        state["tctr"] += 1
                        wv = [(S["dve"], yv)] if dc == 0 else []
                        if state["tguard"][sl]:
                            wv.append(state["tguard"][sl])
                        if yT_war is not None and tcc == 0 and dc == 0:
                            wv.append(yT_war)
                        tv = P.emit("tensor", lambda E, sl=sl, dc=dc, par=par: E.transpose(
                            pts[sl][:],
                            lny[:, par * 1024 + dc * 128:par * 1024 + (dc + 1) * 128],
                            ident_bf[:]), inc=S["pe"], wait=wv)
                        ev2 = P.emit("scalar", lambda E, sl=sl, dc=dc, tcc=tcc: E.copy(
                            out=yT[:, dc * 1024 + tcc * 128:dc * 1024 + tcc * 128 + 128],
                            in_=pts[sl][:]), inc=S["act"], wait=[(S["pe"], tv)])
                        state["tguard"][sl] = (S["act"], ev2)
                        last_ev = ev2
                    state["lny_free"][par] = (S["pe"], tv)
                    ytv.append((S["act"], last_ev))
                return ytv, last_x_read
            return ln_tc

        def emit_ffn(layer, x_in, x_out):
            E_ = 8 if layer == "l0" else 14
            with ExitStack() as ph:
                A = ph.enter_context
                xbuf = A(nc.sbuf_tensor(nm(), [128, 8192], F32))
                lnx = A(nc.sbuf_tensor(nm(), [128, 2048], F32))
                lnsq = A(nc.sbuf_tensor(nm(), [128, 1024], F32))
                lny = A(nc.sbuf_tensor(nm(), [128, 2048], BF16))
                stats = A(nc.sbuf_tensor(nm(), [128, 32], F32))
                yT = A(nc.sbuf_tensor(nm(), [128, 8192], BF16))
                w1b = A(nc.sbuf_tensor(nm(), [128, 16384], BF16))
                w2b = A(nc.sbuf_tensor(nm(), [128, 16384], BF16))
                hT = A(nc.sbuf_tensor(nm(), [128, 16384], BF16))
                stg = A(nc.sbuf_tensor(nm(), [128, 4096], F32))
                gs = A(nc.sbuf_tensor(nm(), [128, 1024], F32))
                rt = A(nc.sbuf_tensor(nm(), [128, 128], F32))
                coefb = A(nc.sbuf_tensor(nm(), [128, 112], F32))
                rtrb = A(nc.sbuf_tensor(nm(), [128, 8 * 14], BF16))
                ps = [A(nc.psum_tensor(nm('q'), [128, 512], F32)) for _ in range(6)]
                pts = [A(nc.psum_tensor(nm('q'), [128, 128], BF16)) for _ in range(2)]
                bguard = {}
                ln_tc = make_ln({"xbuf": xbuf, "lnx": lnx, "lnsq": lnsq, "lny": lny,
                                 "stats": stats, "yT": yT, "pt": pts})

                rv = P.emit("sync", lambda E: E.dma_start(
                    out=stg[:, 0:8 * E_].rearrange("p (a q) -> p a q", q=E_),
                    in_=(wd["L0_ffn_router"] if layer == "l0" else wd["L1_ffn_router_w"])
                    .rearrange("(a p) e -> p a e", p=128)), inc=S["lw"])
                rcv = P.emit("gpsimd", lambda E: E.tensor_copy(
                    out=rtrb[:, 0:8 * E_], in_=stg[:, 0:8 * E_]),
                    inc=S["pool"], wait=[(S["lw"], rv)])
                stg_free = [(S["pool"], rcv), (S["pool"], rcv)]

                if layer == "l0":
                    experts = [("swiglu", 512, e) for e in range(8)] + [("swiglu", 1024, "sh")]
                else:
                    experts = [("gelu", 2048, e) for e in range(14)]

                s1_end_prev = None
                s2_end_prev = None
                hT_free = None
                store_prev = None
                evac_last = None
                for tc in range(nT8):
                    lxv = P.emit("sync", lambda E, tc=tc: E.dma_start(
                        out=xbuf[:].rearrange("p (j d) -> p j d", d=1024),
                        in_=x_in.rearrange("(a j p) d -> a p j d", p=128, j=8)[tc]),
                        inc=S["lx"], wait=[store_prev] if store_prev else [])
                    ytv, _ = ln_tc((S["lx"], lxv), yT_war=s1_end_prev if tc > 0 else None)
                    coef_ready = []
                    for tcc in range(8):
                        o = (tcc % 2) * 64
                        so = (tcc % 2) * 16
                        wv = [ytv[tcc], (S["pool"], rcv)]
                        if ("s", 0) in bguard:
                            wv.append(bguard[("s", 0)])
                        mv = None
                        for dc in range(8):
                            mv = P.emit("tensor", lambda E, dc=dc, tcc=tcc: E.matmul(
                                ps[0][:, 0:E_],
                                yT[:, dc * 1024 + tcc * 128:dc * 1024 + tcc * 128 + 128],
                                rtrb[:, dc * E_:(dc + 1) * E_],
                                start=(dc == 0), stop=(dc == 7)), inc=S["pe"],
                                wait=wv if dc == 0 else [])
                        nv = P.emit("vector", lambda E, so=so: E.tensor_reduce(
                            out=stats[:, so + 5:so + 6], in_=ps[0][:, 0:E_],
                            axis=AX.X, op=ALU.max, negate=True), inc=S["dve"],
                            wait=[(S["pe"], mv)])
                        rt_war = coef_ready[tcc - 2] if tcc >= 2 else None
                        ev = P.emit("scalar", lambda E, o=o, so=so: E.activation(
                            out=rt[:, o:o + E_], in_=ps[0][:, 0:E_], func=AF.Exp,
                            bias=stats[:, so + 5:so + 6]), inc=S["act"],
                            wait=[(S["dve"], nv)] + ([rt_war] if rt_war else []))
                        bguard[("s", 0)] = (S["act"], ev)
                        pr = rt[:, o:o + E_]
                        t1 = rt[:, o + 16:o + 16 + E_]
                        t2 = rt[:, o + 32:o + 32 + E_]
                        P.emit("vector", lambda E, pr=pr, so=so: E.reduce_sum(
                            out=stats[:, so + 11:so + 12], in_=pr, axis=AX.X),
                            wait=[(S["act"], ev)])
                        P.emit("vector", lambda E, so=so: E.reciprocal(
                            out=stats[:, so + 10:so + 11], in_=stats[:, so + 11:so + 12]))
                        P.emit("vector", lambda E, pr=pr, so=so: E.tensor_scalar(
                            out=pr, in0=pr, scalar1=stats[:, so + 10:so + 11],
                            scalar2=None, op0=ALU.mult))
                        P.emit("vector", lambda E, pr=pr, so=so: E.tensor_reduce(
                            out=stats[:, so + 6:so + 7], in_=pr, axis=AX.X, op=ALU.max))
                        P.emit("vector", lambda E, pr=pr, t1=t1, so=so: E.tensor_scalar(
                            out=t1, in0=pr, scalar1=stats[:, so + 6:so + 7],
                            scalar2=None, op0=ALU.is_ge))
                        P.emit("vector", lambda E, t1=t1, so=so: E.reduce_sum(
                            out=stats[:, so + 7:so + 8], in_=t1, axis=AX.X))
                        if layer == "l0":
                            P.emit("vector", lambda E, pr=pr, t1=t1, t2=t2: E.scalar_tensor_tensor(
                                out=t2, in0=t1, scalar=NEG, in1=pr, op0=ALU.mult, op1=ALU.add))
                            P.emit("vector", lambda E, t2=t2, so=so: E.tensor_reduce(
                                out=stats[:, so + 8:so + 9], in_=t2, axis=AX.X, op=ALU.max))
                            P.emit("vector", lambda E, t2=t2, so=so: E.tensor_scalar(
                                out=t2, in0=t2, scalar1=stats[:, so + 8:so + 9],
                                scalar2=None, op0=ALU.is_ge))
                            P.emit("vector", lambda E, so=so: E.tensor_scalar(
                                out=stats[:, so + 9:so + 10], in0=stats[:, so + 7:so + 8],
                                scalar1=1.5, scalar2=None, op0=ALU.is_lt))
                            P.emit("vector", lambda E, t2=t2, so=so: E.tensor_scalar(
                                out=t2, in0=t2, scalar1=stats[:, so + 9:so + 10],
                                scalar2=None, op0=ALU.mult))
                            P.emit("vector", lambda E, t1=t1, t2=t2: E.tensor_tensor(
                                out=t1, in0=t1, in1=t2, op=ALU.add))
                            P.emit("vector", lambda E, t1=t1, pr=pr: E.tensor_tensor(
                                out=t1, in0=t1, in1=pr, op=ALU.mult))
                        else:
                            P.emit("vector", lambda E, so=so: E.reciprocal(
                                out=stats[:, so + 9:so + 10], in_=stats[:, so + 7:so + 8]))
                            P.emit("vector", lambda E, so=so: E.tensor_tensor(
                                out=stats[:, so + 8:so + 9], in0=stats[:, so + 6:so + 7],
                                in1=stats[:, so + 9:so + 10], op=ALU.mult))
                            P.emit("vector", lambda E, t1=t1, so=so: E.tensor_scalar(
                                out=t1, in0=t1, scalar1=stats[:, so + 8:so + 9],
                                scalar2=None, op0=ALU.mult))
                        cv = P.emit("vector", lambda E, t1=t1, tcc=tcc: E.tensor_scalar(
                            out=coefb[:, tcc * E_:(tcc + 1) * E_], in0=t1, scalar1=0.5,
                            scalar2=None, op0=ALU.mult), inc=S["dve"])
                        coef_ready.append((S["dve"], cv))

                    for kind, F_, e in experts:
                        nfc1 = F_ // 128
                        nfc2 = F_ // 128
                        if layer == "l1":
                            w1src = c_l1w1[e].rearrange("a p f -> p a f")
                            w2src = c_l1w2[e].rearrange("a p q -> p a q")
                            w1dst = w1b[:].rearrange("p (a q) -> p a q", q=2048)
                            w2dst = w2b[:].rearrange("p (a q) -> p a q", q=1024)
                        elif e == "sh":
                            w1src = c_sh1.rearrange("a p f -> p a f")
                            w2src = c_sh2.rearrange("a p q -> p a q")
                            w1dst = w1b[:].rearrange("p (a q) -> p a q", q=2048)
                            w2dst = w2b[:, 0:8 * 1024].rearrange("p (a q) -> p a q", q=1024)
                        else:
                            w1src = c_l0w1[e].rearrange("a p f -> p a f")
                            w2src = c_l0w2[e].rearrange("a p q -> p a q")
                            w1dst = w1b[:, 0:8 * 1024].rearrange("p (a q) -> p a q", q=1024)
                            w2dst = w2b[:, 0:4 * 1024].rearrange("p (a q) -> p a q", q=1024)
                        wv1 = [s1_end_prev] if s1_end_prev else []
                        dv1 = P.emit("sync", lambda E, w1src=w1src, w1dst=w1dst: E.dma_start(
                            out=w1dst, in_=w1src), inc=S["lw"], wait=wv1)
                        w1_ready = (S["lw"], dv1)
                        wv2 = [s2_end_prev] if s2_end_prev else []
                        dv2 = P.emit("sync", lambda E, w2src=w2src, w2dst=w2dst: E.dma_start(
                            out=w2dst, in_=w2src), inc=S["lw2"], wait=wv2)
                        w2_ready = (S["lw2"], dv2)

                        cnt = 0
                        s1_last = None
                        h_last = None
                        for fc in range(nfc1):
                            for th in range(2):
                                if kind == "gelu":
                                    bank = cnt % 2
                                    pp = ps[bank]
                                    wv = []
                                    if cnt == 0:
                                        wv = [w1_ready, ytv[-1]]
                                        if hT_free is not None:
                                            wv.append(hT_free)
                                    if ("s", bank) in bguard:
                                        wv.append(bguard[("s", bank)])
                                    mv = None
                                    for dc in range(8):
                                        mv = P.emit("tensor", lambda E, pp=pp, dc=dc, fc=fc, th=th: E.matmul(
                                            pp[:],
                                            w1b[:, dc * 2048 + fc * 128:dc * 2048 + (fc + 1) * 128],
                                            yT[:, dc * 1024 + th * 512:dc * 1024 + (th + 1) * 512],
                                            start=(dc == 0), stop=(dc == 7)), inc=S["pe"],
                                            wait=wv if dc == 0 else [])
                                    s1_last = mv
                                    ev = P.emit("scalar", lambda E, pp=pp, fc=fc, th=th, e=e: E.activation(
                                        out=hT[:, fc * 1024 + th * 512:fc * 1024 + (th + 1) * 512],
                                        in_=pp[:], func=AF.Gelu,
                                        bias=b1T[:, fc * 14 + e:fc * 14 + e + 1]),
                                        inc=S["act"], wait=[(S["pe"], mv)])
                                    bguard[("s", bank)] = (S["act"], ev)
                                    h_last = (S["act"], ev)
                                else:
                                    bg, bu = (0, 1) if cnt % 2 == 0 else (4, 5)
                                    ppg, ppu = ps[bg], ps[bu]
                                    qw = 2048 if e == "sh" else 1024
                                    goff, uoff = 0, (1024 if e == "sh" else 512)
                                    wv = []
                                    if cnt == 0:
                                        wv = [w1_ready, ytv[-1]]
                                        if hT_free is not None:
                                            wv.append(hT_free)
                                    if ("s", bg) in bguard:
                                        wv.append(bguard[("s", bg)])
                                    mgv = None
                                    for dc in range(8):
                                        mgv = P.emit("tensor", lambda E, ppg=ppg, dc=dc, fc=fc, th=th, qw=qw, goff=goff: E.matmul(
                                            ppg[:],
                                            w1b[:, dc * qw + goff + fc * 128:dc * qw + goff + (fc + 1) * 128],
                                            yT[:, dc * 1024 + th * 512:dc * 1024 + (th + 1) * 512],
                                            start=(dc == 0), stop=(dc == 7)), inc=S["pe"],
                                            wait=wv if dc == 0 else [])
                                    wv2 = []
                                    if ("s", bu) in bguard:
                                        wv2.append(bguard[("s", bu)])
                                    muv = None
                                    for dc in range(8):
                                        muv = P.emit("tensor", lambda E, ppu=ppu, dc=dc, fc=fc, th=th, qw=qw, uoff=uoff: E.matmul(
                                            ppu[:],
                                            w1b[:, dc * qw + uoff + fc * 128:dc * qw + uoff + (fc + 1) * 128],
                                            yT[:, dc * 1024 + th * 512:dc * 1024 + (th + 1) * 512],
                                            start=(dc == 0), stop=(dc == 7)), inc=S["pe"],
                                            wait=wv2 if dc == 0 else [])
                                    s1_last = muv
                                    gpar = cnt % 2
                                    sv = P.emit("scalar", lambda E, ppg=ppg, gpar=gpar: E.activation(
                                        out=gs[:, gpar * 512:(gpar + 1) * 512], in_=ppg[:],
                                        func=AF.Silu), inc=S["act"], wait=[(S["pe"], mgv)])
                                    bguard[("s", bg)] = (S["act"], sv)
                                    hv = P.emit("vector", lambda E, ppu=ppu, gpar=gpar, fc=fc, th=th: E.tensor_tensor(
                                        out=hT[:, fc * 1024 + th * 512:fc * 1024 + (th + 1) * 512],
                                        in0=gs[:, gpar * 512:(gpar + 1) * 512], in1=ppu[:],
                                        op=ALU.mult), inc=S["dve"],
                                        wait=[(S["act"], sv), (S["pe"], muv)])
                                    bguard[("s", bu)] = (S["dve"], hv)
                                    h_last = (S["dve"], hv)
                                cnt += 1
                        s1_end_prev = (S["pe"], s1_last)

                        s2_last = None
                        for tcc in range(8):
                            for half in range(2):
                                bank = 2 + half
                                pp = ps[bank]
                                wv = [h_last]
                                if w2_ready is not None and tcc == 0 and half == 0:
                                    wv.append(w2_ready)
                                if ("s", bank) in bguard:
                                    wv.append(bguard[("s", bank)])
                                mv = None
                                for fc in range(nfc2):
                                    mv = P.emit("tensor", lambda E, pp=pp, fc=fc, tcc=tcc, half=half, nfc2=nfc2, layer=layer: E.matmul(
                                        pp[:],
                                        hT[:, fc * 1024 + tcc * 128:fc * 1024 + (tcc + 1) * 128],
                                        w2b[:, fc * 1024 + half * 512:fc * 1024 + (half + 1) * 512],
                                        start=(fc == 0), stop=(fc == nfc2 - 1 and layer != "l1")),
                                        inc=S["pe"], wait=wv if fc == 0 else [])
                                if layer == "l1":
                                    bb = 32 * (e % 4)
                                    bcol = (e // 4) * 1024 + half * 512
                                    mv = P.emit("tensor", lambda E, pp=pp, bb=bb, bcol=bcol: E.matmul(
                                        pp[:], ones_t[bb:bb + 1, 0:128],
                                        b2p[bb:bb + 1, bcol:bcol + 512],
                                        start=False, stop=True, tile_position=(bb, 0)),
                                        inc=S["pe"])
                                s2_last = mv
                                scal = 0.5 if e == "sh" else coefb[:, tcc * E_ + e:tcc * E_ + e + 1]
                                xsl = xbuf[:, tcc * 1024 + half * 512:tcc * 1024 + (half + 1) * 512]
                                dvv = P.emit("vector", lambda E, pp=pp, scal=scal, xsl=xsl: E.scalar_tensor_tensor(
                                    out=xsl, in0=pp[:], scalar=scal, in1=xsl,
                                    op0=ALU.mult, op1=ALU.add), inc=S["dve"],
                                    wait=[(S["pe"], mv), coef_ready[tcc]])
                                bguard[("s", bank)] = (S["dve"], dvv)
                                evac_last = (S["dve"], dvv)
                        s2_end_prev = (S["pe"], s2_last)
                        hT_free = (S["pe"], s2_last)

                    sv = P.emit("scalar", lambda E, tc=tc: E.dma_start(
                        out=x_out.rearrange("(a j p) d -> a p j d", p=128, j=8)[tc],
                        in_=xbuf[:].rearrange("p (j d) -> p j d", d=1024)),
                        inc=S["st"], wait=[evac_last], dma=True)
                    store_prev = (S["st"], sv)

        def emit_attn(layer, x_in, x_out):
            is0 = layer == "l0"
            HPG = 8 if is0 else 16
            NG = 2 if is0 else 1
            HD = 64 if is0 else 32
            VW = 64 if is0 else 32
            VS = VW + 1
            vs_tile = HPG * VS
            GW = HPG * VW
            with ExitStack() as ph:
                A = ph.enter_context
                kt = [A(nc.sbuf_tensor(nm(), [128, T], BF16)) for _ in range(4)]
                vt = A(nc.sbuf_tensor(nm(), [128, (T // 128) * vs_tile], BF16))
                stats = A(nc.sbuf_tensor(nm(), [128, 32], F32))
                wot_g = A(nc.sbuf_tensor(nm(), [128, 4096], BF16))
                stg = A(nc.sbuf_tensor(nm(), [128, 2048], F32))

                if not is0:
                    wq_b = A(nc.sbuf_tensor(nm(), [128, 8192], BF16))
                    wkv_b = A(nc.sbuf_tensor(nm(), [128, 1024], BF16))
                    wk_b = A(nc.sbuf_tensor(nm(), [128, 512], BF16))
                    wv_b = A(nc.sbuf_tensor(nm(), [128, 512], BF16))
                    sfree = None
                    jobs = []
                    wqs = wd["L1_attn_Wq"].rearrange("(a p) n -> p a n", p=128)
                    for j in range(8):
                        jobs.append((wqs[:, j, :], wq_b[:, j * 1024:(j + 1) * 1024], None, 1024))
                    jobs.append((wd["L1_attn_Wkv"].rearrange("(a p) l -> p a l", p=128),
                                 wkv_b[:].rearrange("p (a l) -> p a l", l=128), 128, 1024))
                    jobs.append((wd["L1_attn_Wk"], wk_b[:], None, 512))
                    jobs.append((wd["L1_attn_Wv"], wv_b[:], None, 512))
                    for src, dst, q, width in jobs:
                        sgv = stg[:, 0:width]
                        if q is not None:
                            sgv = sgv.rearrange("p (a q) -> p a q", q=q)
                        dv = P.emit("sync", lambda E, src=src, sgv=sgv: E.dma_start(
                            out=sgv, in_=src), inc=S["lw"],
                            wait=[sfree] if sfree else [])
                        sin = stg[:, 0:width]
                        if q is not None:
                            sin = sin.rearrange("p (a q) -> p a q", q=q)
                        cvv = P.emit("gpsimd", lambda E, sin=sin, dst=dst: E.tensor_copy(
                            out=dst, in_=sin), inc=S["pool"], wait=[(S["lw"], dv)])
                        sfree = (S["pool"], cvv)
                    wq_ready = sfree
                    for oc in range(4):
                        dvs = None
                        for hh in range(4):
                            h = oc * 4 + hh
                            dvs = P.emit("sync", lambda E, h=h, hh=hh: E.dma_start(
                                out=stg[32 * hh:32 * hh + 32, 0:1024],
                                in_=wd["L1_attn_Wo"][h * 64:h * 64 + 32, :]), inc=S["lw"],
                                wait=[sfree] if hh == 0 else [])
                        cvv = P.emit("gpsimd", lambda E, oc=oc: E.tensor_copy(
                            out=wot_g[:, oc * 1024:(oc + 1) * 1024],
                            in_=stg[:, 0:1024]), inc=S["pool"], wait=[(S["lw"], dvs)])
                        sfree = (S["pool"], cvv)
                    wo_ready = sfree

                def passA(g):
                    with ExitStack() as pa:
                        B_ = pa.enter_context
                        psS = [B_(nc.psum_tensor(nm('q'), [128, 512], F32)) for _ in range(2)]
                        psO = [B_(nc.psum_tensor(nm('q'), [128, 512], F32)) for _ in range(2)]
                        pstA = B_(nc.psum_tensor(nm('q'), [128, 128], BF16))
                        pstB = B_(nc.psum_tensor(nm('q'), [128, 128], BF16))
                        xbuf = B_(nc.sbuf_tensor(nm(), [128, 8192], F32))
                        lnx = B_(nc.sbuf_tensor(nm(), [128, 2048], F32))
                        lnsq = B_(nc.sbuf_tensor(nm(), [128, 1024], F32))
                        lny = B_(nc.sbuf_tensor(nm(), [128, 2048], BF16))
                        yT = B_(nc.sbuf_tensor(nm(), [128, 8192], BF16))
                        qst = B_(nc.sbuf_tensor(nm(), [128, 4096], BF16))
                        if is0:
                            wcur = B_(nc.sbuf_tensor(nm(), [128, 4096], BF16))
                        else:
                            kvT = B_(nc.sbuf_tensor(nm(), [128, 1024], BF16))
                            mtp0 = B_(nc.sbuf_tensor(nm(), [128, 512], F32))
                            qrope = B_(nc.sbuf_tensor(nm(), [128, 1024], BF16))
                            knb = B_(nc.sbuf_tensor(nm(), [128, 1024], BF16))
                            sqf = B_(nc.sbuf_tensor(nm(), [128, 512], F32))
                            rms = B_(nc.sbuf_tensor(nm(), [128, 96], F32))
                        bguard = {}
                        tg = [None, None]
                        ln_tc = make_ln({"xbuf": xbuf, "lnx": lnx, "lnsq": lnsq,
                                         "lny": lny, "stats": stats, "yT": yT,
                                         "pt": [pstA, pstB], "tguard": tg})
                        qst_free = None
                        x_read_prev = None
                        wcur_guard = None
                        for tc in range(nT8):
                            lxv = P.emit("sync", lambda E, tc=tc: E.dma_start(
                                out=xbuf[:].rearrange("p (j d) -> p j d", d=1024),
                                in_=x_in.rearrange("(a j p) d -> a p j d", p=128, j=8)[tc]),
                                inc=S["lx"], wait=[x_read_prev] if x_read_prev else [])
                            ytv, x_read_prev = ln_tc((S["lx"], lxv))
                            yT_all = ytv[-1]
                            if is0:
                                for which, cache in (("q", "wq_c"), ("k", "wk_c")):
                                    wlv = P.emit("sync", lambda E, cache=cache, g=g: E.dma_start(
                                        out=wcur[:].rearrange("p (i m) -> p i m", m=512),
                                        in_=wcaches[cache].rearrange("i p m -> p i m")[:, :, g * 512:(g + 1) * 512]),
                                        inc=S["lq"], wait=[wcur_guard] if wcur_guard else [])
                                    for dl in range(4):
                                        for th in range(2):
                                            bank = (dl * 2 + th) % 2
                                            pp = psS[bank]
                                            wv = [yT_all, (S["lq"], wlv)] if (dl == 0 and th == 0) else []
                                            if ("pj", bank) in bguard:
                                                wv.append(bguard[("pj", bank)])
                                            mv = None
                                            for mc in range(8):
                                                mv = P.emit("tensor", lambda E, pp=pp, mc=mc, dl=dl, th=th: E.matmul(
                                                    pp[:],
                                                    wcur[:, mc * 512 + dl * 128:mc * 512 + (dl + 1) * 128],
                                                    yT[:, mc * 1024 + th * 512:mc * 1024 + (th + 1) * 512],
                                                    start=(mc == 0), stop=(mc == 7)), inc=S["pe"],
                                                    wait=wv if mc == 0 else [])
                                            wcur_guard = (S["pe"], mv)
                                            if which == "q":
                                                wv2 = [(S["pe"], mv)]
                                                if qst_free is not None and dl == 0 and th == 0:
                                                    wv2.append(qst_free)
                                                ev = P.emit("scalar", lambda E, pp=pp, dl=dl, th=th: E.copy(
                                                    out=qst[:, dl * 1024 + th * 512:dl * 1024 + (th + 1) * 512],
                                                    in_=pp[:]), inc=S["act"], wait=wv2)
                                            else:
                                                ev = P.emit("scalar", lambda E, pp=pp, dl=dl, th=th, tc=tc: E.copy(
                                                    out=kt[dl][:, tc * 1024 + th * 512:tc * 1024 + (th + 1) * 512],
                                                    in_=pp[:]), inc=S["act"], wait=[(S["pe"], mv)])
                                            bguard[("pj", bank)] = (S["act"], ev)
                                    if which == "q":
                                        qsv = P.emit("scalar", lambda E, tc=tc, g=g: E.dma_start(
                                            out=qspill[g * 4:g * 4 + 4, :, tc * 1024:(tc + 1) * 1024]
                                            .rearrange("i p t -> p i t"),
                                            in_=qst[:].rearrange("p (i t) -> p i t", t=1024)),
                                            inc=S["st"], wait=[(S["act"], S["act"].n)], dma=True)
                                        qst_free = (S["st"], qsv)
                                wlv = P.emit("sync", lambda E, g=g: E.dma_start(
                                    out=wcur[:].rearrange("p (i m) -> p i m", m=512),
                                    in_=wcaches["wv_c"].rearrange("i p m -> p i m")[:, :, g * 512:(g + 1) * 512]),
                                    inc=S["lq"], wait=[wcur_guard] if wcur_guard else [])
                                for tcc in range(8):
                                    bank = tcc % 2
                                    pp = psO[bank]
                                    wv = [yT_all, (S["lq"], wlv)] if tcc == 0 else []
                                    if ("v", bank) in bguard:
                                        wv.append(bguard[("v", bank)])
                                    mv = None
                                    for mc in range(8):
                                        mv = P.emit("tensor", lambda E, pp=pp, mc=mc, tcc=tcc: E.matmul(
                                            pp[:],
                                            yT[:, mc * 1024 + tcc * 128:mc * 1024 + (tcc + 1) * 128],
                                            wcur[:, mc * 512:(mc + 1) * 512],
                                            start=(mc == 0), stop=(mc == 7)), inc=S["pe"],
                                            wait=wv if mc == 0 else [])
                                    wcur_guard = (S["pe"], mv)
                                    ti = tc * 8 + tcc
                                    P.emit("vector", lambda E, pp=pp, ti=ti: E.tensor_copy(
                                        out=vt[:, ti * vs_tile:(ti + 1) * vs_tile]
                                        .rearrange("p (h s) -> p h s", s=VS)[:, :, 0:VW],
                                        in_=pp[:].rearrange("p (h s) -> p h s", s=VW)),
                                        wait=[(S["pe"], mv)])
                                    vv = P.emit("vector", lambda E, ti=ti: E.memset(
                                        vt[:, ti * vs_tile:(ti + 1) * vs_tile]
                                        .rearrange("p (h s) -> p h s", s=VS)[:, :, VW:VS], 1.0),
                                        inc=S["dve"])
                                    bguard[("v", bank)] = (S["dve"], vv)
                            else:
                                kv_ready = None
                                for th in range(2):
                                    bank = th
                                    pp = psS[bank]
                                    wv = [yT_all, wq_ready] if th == 0 else []
                                    if ("kv", bank) in bguard:
                                        wv.append(bguard[("kv", bank)])
                                    mv = None
                                    for dc in range(8):
                                        mv = P.emit("tensor", lambda E, pp=pp, dc=dc, th=th: E.matmul(
                                            pp[:], wkv_b[:, dc * 128:(dc + 1) * 128],
                                            yT[:, dc * 1024 + th * 512:dc * 1024 + (th + 1) * 512],
                                            start=(dc == 0), stop=(dc == 7)), inc=S["pe"],
                                            wait=wv if dc == 0 else [])
                                    ev = P.emit("scalar", lambda E, pp=pp, th=th: E.copy(
                                        out=kvT[:, th * 512:(th + 1) * 512], in_=pp[:]),
                                        inc=S["act"], wait=[(S["pe"], mv)])
                                    bguard[("kv", bank)] = (S["act"], ev)
                                    kv_ready = (S["act"], ev)
                                for tcc in range(8):
                                    par = tcc % 2
                                    qdone = None
                                    for qh in range(2 if "pa_no_q" not in DBG else 0):
                                        bank = qh
                                        pp = psS[bank]
                                        wv = [kv_ready]
                                        if ("kv", bank) in bguard:
                                            wv.append(bguard[("kv", bank)])
                                        mv = None
                                        for dc in range(8):
                                            mv = P.emit("tensor", lambda E, pp=pp, dc=dc, tcc=tcc, qh=qh: E.matmul(
                                                pp[:],
                                                yT[:, dc * 1024 + tcc * 128:dc * 1024 + (tcc + 1) * 128],
                                                wq_b[:, dc * 1024 + qh * 512:dc * 1024 + (qh + 1) * 512],
                                                start=(dc == 0), stop=(dc == 7)), inc=S["pe"],
                                                wait=wv if dc == 0 else [])
                                        sv = P.emit("scalar", lambda E, pp=pp: E.activation(
                                            out=sqf[:], in_=pp[:], func=AF.Square),
                                            inc=S["act"], wait=[(S["pe"], mv)])
                                        rv = P.emit("vector", lambda E, qh=qh: E.tensor_reduce(
                                            out=rms[:, qh * 8:qh * 8 + 8].rearrange("p (h s) -> p h s", s=1),
                                            in_=sqf[:].rearrange("p (h s) -> p h s", s=64),
                                            axis=AX.X, op=ALU.add), inc=S["dve"],
                                            wait=[(S["act"], sv)])
                                        P.emit("scalar", lambda E, qh=qh: E.activation(
                                            out=rms[:, 16 + qh * 8:16 + qh * 8 + 8],
                                            in_=rms[:, qh * 8:qh * 8 + 8],
                                            func=AF.Ln, bias=1e-6, scale=1.0 / 64),
                                            wait=[(S["dve"], rv)])
                                        ev2 = P.emit("scalar", lambda E, qh=qh: E.activation(
                                            out=rms[:, 32 + qh * 8:32 + qh * 8 + 8],
                                            in_=rms[:, 16 + qh * 8:16 + qh * 8 + 8],
                                            func=AF.Exp, scale=-0.5), inc=S["act"])
                                        gv = None
                                        for j in range(8):
                                            h = qh * 8 + j
                                            wgv = [(S["act"], ev2)] if j == 0 else []
                                            if j == 0 and qh == 0 and ("qrope", par) in bguard:
                                                wgv.append(bguard[("qrope", par)])
                                            gv = P.emit("vector", lambda E, pp=pp, j=j, h=h, par=par: E.scalar_tensor_tensor(
                                                out=qrope[:, par * 512 + h * 32:par * 512 + (h + 1) * 32],
                                                in0=pp[:, j * 64:j * 64 + 32],
                                                scalar=rms[:, 32 + h:32 + h + 1],
                                                in1=qgb[:, h * 64:h * 64 + 32],
                                                op0=ALU.mult, op1=ALU.mult), inc=S["dve"],
                                                wait=wgv)
                                        bguard[("kv", bank)] = (S["dve"], gv)
                                        qdone = (S["dve"], gv)
                                    for hg in range(4 if ("pa_no_q" not in DBG and "pa_no_qt" not in DBG) else 0):
                                        sl = hg % 2
                                        pslot = (pstA, pstB)[sl]
                                        wv = [qdone] if hg == 0 else []
                                        if tg[sl]:
                                            wv.append(tg[sl])
                                        if qst_free is not None and hg == 0 and tcc == 0:
                                            wv.append(qst_free)
                                        tvq = P.emit("tensor", lambda E, pslot=pslot, hg=hg, par=par: E.transpose(
                                            pslot[:],
                                            qrope[:, par * 512 + hg * 128:par * 512 + (hg + 1) * 128],
                                            ident_bf[:]), inc=S["pe"], wait=wv)
                                        evq = P.emit("scalar", lambda E, pslot=pslot, hg=hg, tcc=tcc: E.copy(
                                            out=qst[:, hg * 1024 + tcc * 128:hg * 1024 + (tcc + 1) * 128],
                                            in_=pslot[:]), inc=S["act"],
                                            wait=[(S["pe"], tvq)])
                                        tg[sl] = (S["act"], evq)
                                        bguard[("qrope", par)] = (S["pe"], tvq)
                                    if "pa_no_v" in DBG:
                                        continue
                                    pp = psS[1]
                                    wv = []
                                    if ("kv", 1) in bguard:
                                        wv.append(bguard[("kv", 1)])
                                    mv = P.emit("tensor", lambda E, pp=pp, tcc=tcc: E.matmul(
                                        pp[:], kvT[:, tcc * 128:(tcc + 1) * 128], wv_b[:],
                                        start=True, stop=True), inc=S["pe"], wait=wv)
                                    ti = tc * 8 + tcc
                                    if "pa_v_noevac" in DBG:
                                        vv = P.emit("vector", lambda E, pp=pp: E.tensor_copy(
                                            out=mtp0[:], in_=pp[:]), inc=S["dve"],
                                            wait=[(S["pe"], mv)])
                                        bguard[("kv", 1)] = (S["dve"], vv)
                                        v_done = (S["dve"], vv)
                                        continue
                                    P.emit("vector", lambda E, pp=pp, ti=ti: E.tensor_copy(
                                        out=vt[:, ti * vs_tile:(ti + 1) * vs_tile]
                                        .rearrange("p (h s) -> p h s", s=VS)[:, :, 0:VW],
                                        in_=pp[:].rearrange("p (h s) -> p h s", s=VW)),
                                        wait=[(S["pe"], mv)])
                                    vv = P.emit("vector", lambda E, ti=ti: E.memset(
                                        vt[:, ti * vs_tile:(ti + 1) * vs_tile]
                                        .rearrange("p (h s) -> p h s", s=VS)[:, :, VW:VS], 1.0),
                                        inc=S["dve"])
                                    bguard[("kv", 1)] = (S["dve"], vv)
                                    if "pa_no_k" in DBG:
                                        continue
                                    pp = psS[0]
                                    wv = []
                                    if ("kv", 0) in bguard:
                                        wv.append(bguard[("kv", 0)])
                                    mv = P.emit("tensor", lambda E, pp=pp, tcc=tcc: E.matmul(
                                        pp[:], kvT[:, tcc * 128:(tcc + 1) * 128], wk_b[:],
                                        start=True, stop=True), inc=S["pe"], wait=wv)
                                    if "pa_k_notp" in DBG:
                                        pass
                                    if "pa_k_simple" in DBG:
                                        gvx = P.emit("scalar", lambda E, pp=pp: E.copy(
                                            out=knb[:, 0:512], in_=pp[:]),
                                            inc=S["act"], wait=[(S["pe"], mv)])
                                        bguard[("kv", 0)] = (S["act"], gvx)
                                        continue
                                    sv = P.emit("scalar", lambda E, pp=pp: E.activation(
                                        out=sqf[:], in_=pp[:], func=AF.Square),
                                        inc=S["act"], wait=[(S["pe"], mv)])
                                    rv = P.emit("vector", lambda E: E.tensor_reduce(
                                        out=rms[:, 48:64].rearrange("p (h s) -> p h s", s=1),
                                        in_=sqf[:].rearrange("p (h s) -> p h s", s=32),
                                        axis=AX.X, op=ALU.add), inc=S["dve"],
                                        wait=[(S["act"], sv)])
                                    P.emit("scalar", lambda E: E.activation(
                                        out=rms[:, 64:80], in_=rms[:, 48:64],
                                        func=AF.Ln, bias=1e-6, scale=1.0 / 32),
                                        wait=[(S["dve"], rv)])
                                    ev2 = P.emit("scalar", lambda E: E.activation(
                                        out=rms[:, 80:96], in_=rms[:, 64:80],
                                        func=AF.Exp, scale=-0.5), inc=S["act"])
                                    gv = None
                                    for h in range(16):
                                        if "pa_k_nostt" in DBG:
                                            gv = P.emit("vector", lambda E, pp=pp: E.tensor_copy(
                                                out=mtp0[:], in_=pp[:]), inc=S["dve"])
                                            break
                                        wgv = [(S["act"], ev2)] if h == 0 else []
                                        if h == 0 and ("knb", par) in bguard:
                                            wgv.append(bguard[("knb", par)])
                                        gv = P.emit("vector", lambda E, pp=pp, h=h, par=par: E.scalar_tensor_tensor(
                                            out=knb[:, par * 512 + h * 32:par * 512 + (h + 1) * 32],
                                            in0=pp[:, h * 32:(h + 1) * 32],
                                            scalar=rms[:, 80 + h:80 + h + 1],
                                            in1=kgb[:, h * 32:(h + 1) * 32],
                                            op0=ALU.mult, op1=ALU.mult), inc=S["dve"],
                                            wait=wgv)
                                    bguard[("kv", 0)] = (S["dve"], gv)
                                    for hg in range(4 if "pa_k_notp" not in DBG and "pa_k_nostt" not in DBG else 0):
                                        sl = hg % 2
                                        pslot = (pstA, pstB)[sl]
                                        wv = [(S["dve"], gv)] if hg == 0 else []
                                        if tg[sl]:
                                            wv.append(tg[sl])
                                        tvk = P.emit("tensor", lambda E, pslot=pslot, hg=hg, par=par: E.transpose(
                                            pslot[:],
                                            knb[:, par * 512 + hg * 128:par * 512 + (hg + 1) * 128],
                                            ident_bf[:]), inc=S["pe"], wait=wv)
                                        kdst = (qst[:, hg * 1024 + tcc * 128:hg * 1024 + (tcc + 1) * 128]
                                                if "pa_k_tp2qst" in DBG else
                                                kt[hg][:, tc * 1024 + tcc * 128:tc * 1024 + (tcc + 1) * 128])
                                        evk = P.emit("scalar", lambda E, pslot=pslot, kdst=kdst: E.copy(
                                            out=kdst,
                                            in_=pslot[:]), inc=S["act"],
                                            wait=[(S["pe"], tvk)])
                                        tg[sl] = (S["act"], evk)
                                        bguard[("knb", par)] = (S["pe"], tvk)
                                qsv = P.emit("scalar", lambda E, tc=tc: E.dma_start(
                                    out=qrspill[:, :, tc * 1024:(tc + 1) * 1024].rearrange("i p t -> p i t"),
                                    in_=qst[:].rearrange("p (i t) -> p i t", t=1024)),
                                    inc=S["st"], wait=[(S["act"], S["act"].n)], dma=True)
                                qst_free = (S["st"], qsv)

                def passB(g):
                    with ExitStack() as pb:
                        B_ = pb.enter_context
                        masks = B_(nc.sbuf_tensor(nm(), [128, 2048], F32))
                        negm = B_(nc.sbuf_tensor(nm(), [128, 2048], BF16))
                        qcur = B_(nc.sbuf_tensor(nm(), [128, 1024], BF16))
                        PT = B_(nc.sbuf_tensor(nm(), [128, 1024], BF16))
                        oq = B_(nc.sbuf_tensor(nm(), [128, 2048], BF16))
                        otb = B_(nc.sbuf_tensor(nm(), [128, 512], BF16))
                        xq = B_(nc.sbuf_tensor(nm(), [128, 2048], F32))
                        mcb = B_(nc.sbuf_tensor(nm(), [128, 8], BF16))
                        mtmp = B_(nc.sbuf_tensor(nm(), [128, 512], F32))
                        ptmp = B_(nc.sbuf_tensor(nm(), [128, 1024], F32))
                        psS = [B_(nc.psum_tensor(nm('q'), [128, 512], F32)) for _ in range(2)]
                        psOq = [B_(nc.psum_tensor(nm('q'), [128, 512], F32)) for _ in range(4)]
                        psM = B_(nc.psum_tensor(nm('q'), [128, 128], F32))
                        pstT = B_(nc.psum_tensor(nm('q'), [128, 128], BF16))
                        bguard = {}
                        cmv = P.emit("sync", lambda E: E.dma_start(out=masks[:], in_=cmask_d),
                                     inc=S["lx"])
                        if is0:
                            wov = P.emit("sync", lambda E, g=g: E.dma_start(
                                out=wot_g[:].rearrange("p (i m) -> p i m", m=1024),
                                in_=wcaches["wo_c"].rearrange("i p m -> p i m")[:, g * 4:g * 4 + 4, :]),
                                inc=S["lx"])
                            wo_ready_g = (S["lx"], wov)
                        else:
                            wo_ready_g = wo_ready
                        negm_guard = {}
                        aw_guard = {}
                        oq_free = None

                        def mprep(hl, tsb):
                            if "no_mprep" in DBG:
                                nv = P.emit("vector", lambda E: E.memset(negm[:, 0:1], 0.0), inc=S["dve"])
                                return (S["dve"], nv), (S["dve"], nv)
                            par = hl % 2
                            if is0:
                                b = 64 * (hl % 2)
                                ktile = kt[hl // 2]
                                src = qspill[g * 4 + hl // 2, b:b + HD, tsb * 512:(tsb + 1) * 512]
                                ncol = (hl // 2) * 512
                            else:
                                b = 32 * (hl % 4)
                                ktile = kt[hl // 4]
                                src = qrspill[hl // 4, b:b + HD, tsb * 512:(tsb + 1) * 512]
                                ncol = (hl // 4) * 512
                            wv = []
                            if ("qc", par) in bguard:
                                wv.append(bguard[("qc", par)])
                            qv = P.emit("sync", lambda E, src=src, par=par, b=b: E.dma_start(
                                out=qcur[b:b + HD, par * 512:(par + 1) * 512], in_=src),
                                inc=S["lq"], wait=wv)
                            nmv = None
                            for qt in range(4):
                                qt_abs = tsb * 4 + qt
                                mcol = par * 4 + qt
                                wv = [(S["lq"], qv)]
                                if "psM" in bguard:
                                    wv.append(bguard["psM"])
                                mv = P.emit("tensor", lambda E, b=b, par=par, qt=qt, qt_abs=qt_abs, ktile=ktile: E.matmul(
                                    psM[:], qcur[b:b + HD, par * 512 + qt * 128:par * 512 + (qt + 1) * 128],
                                    ktile[b:b + HD, qt_abs * 128:(qt_abs + 1) * 128],
                                    start=True, stop=True, tile_position=(b, 0)),
                                    inc=S["pe"], wait=wv)
                                rv = P.emit("vector", lambda E, mcol=mcol: E.tensor_reduce(
                                    out=stats[:, mcol:mcol + 1], in_=psM[:], axis=AX.X,
                                    op=ALU.max, negate=True), inc=S["dve"],
                                    wait=[(S["pe"], mv)])
                                wv2 = [(S["dve"], rv)]
                                if ("mst", mcol) in bguard:
                                    wv2.append(bguard[("mst", mcol)])
                                cv_ = P.emit("vector", lambda E, mcol=mcol: E.tensor_copy(
                                    out=mcb[:, mcol:mcol + 1], in_=stats[:, mcol:mcol + 1]),
                                    inc=S["dve"], wait=wv2)
                                mrv = P.emit("tensor", lambda E, mcol=mcol: E.matmul(
                                    psM[0:1, 0:128], mcb[:, mcol:mcol + 1], ident_bf[:],
                                    start=True, stop=True), inc=S["pe"],
                                    wait=[(S["dve"], cv_)])
                                bguard[("mst", mcol)] = (S["pe"], mrv)
                                wv4 = [(S["pe"], mrv)]
                                if hl in negm_guard:
                                    wv4.append(negm_guard[hl])
                                nmv = P.emit("scalar", lambda E, b=b, ncol=ncol, qt=qt: E.copy(
                                    out=negm[b:b + 1, ncol + qt * 128:ncol + (qt + 1) * 128],
                                    in_=psM[0:1, 0:128]), inc=S["act"], wait=wv4)
                                bguard["psM"] = (S["act"], nmv)
                            return (S["act"], nmv), (S["lq"], qv)

                        def sloop(hl, tsb, ready):
                            if "no_scores" in DBG:
                                nv = P.emit("vector", lambda E: E.memset(oq[:, 0:1], 0.0), inc=S["dve"])
                                return (S["dve"], nv)
                            negm_r, q_r = ready
                            par = hl % 2
                            if is0:
                                b = 64 * (hl % 2)
                                ktile = kt[hl // 2]
                                ncol = (hl // 2) * 512
                            else:
                                b = 32 * (hl % 4)
                                ktile = kt[hl // 4]
                                ncol = (hl // 4) * 512
                            pv_stop = {}
                            pv_last = None
                            for sb in range(tsb * 4 + 4):
                                pp = psS[sb % 2]
                                wv = [negm_r, q_r] if sb < 2 else []
                                if ("S", sb % 2) in bguard:
                                    wv.append(bguard[("S", sb % 2)])
                                if sb == 0:
                                    for qt in range(4):
                                        if ("O", qt) in bguard:
                                            wv.append(bguard[("O", qt)])
                                P.emit("tensor", lambda E, pp=pp, b=b, sb=sb, ktile=ktile, par=par: E.matmul(
                                    pp[:], ktile[b:b + HD, sb * 128:(sb + 1) * 128],
                                    qcur[b:b + HD, par * 512:(par + 1) * 512],
                                    start=True, stop=False, tile_position=(b, 0)), wait=wv)
                                mv = P.emit("tensor", lambda E, pp=pp, b=b, ncol=ncol: E.matmul(
                                    pp[:], ones_t[b:b + 1, 0:128],
                                    negm[b:b + 1, ncol:ncol + 512],
                                    start=False, stop=True, tile_position=(b, 0)), inc=S["pe"])
                                ready_ = (S["pe"], mv)
                                exp_src = pp
                                if sb >= tsb * 4:
                                    k_ = sb - tsb * 4
                                    wvm = [ready_, (S["lx"], cmv)]
                                    if "mtmp" in bguard:
                                        wvm.append(bguard["mtmp"])
                                    dv = P.emit("vector", lambda E, pp=pp, k_=k_: E.tensor_tensor(
                                        out=mtmp[:], in0=pp[:], in1=masks[:, k_ * 512:(k_ + 1) * 512],
                                        op=ALU.add), inc=S["dve"], wait=wvm)
                                    ready_ = (S["dve"], dv)
                                    exp_src = mtmp
                                wv2 = [ready_]
                                if ("PT", sb % 2) in bguard:
                                    wv2.append(bguard[("PT", sb % 2)])
                                ev = P.emit("scalar", lambda E, exp_src=exp_src, sb=sb: E.activation(
                                    out=PT[:, (sb % 2) * 512:(sb % 2 + 1) * 512],
                                    in_=exp_src[:] if exp_src is mtmp else exp_src[:],
                                    func=AF.Exp, scale=0.125), inc=S["act"], wait=wv2)
                                bguard[("S", sb % 2)] = (S["act"], ev)
                                if exp_src is mtmp:
                                    bguard["mtmp"] = (S["act"], ev)
                                for qt in range(max(0, sb - tsb * 4), 4):
                                    if "no_pv" in DBG:
                                        pv_stop[qt] = bguard[("S", sb % 2)]
                                        pv_last = mv
                                        continue
                                    qt_abs = tsb * 4 + qt
                                    wv3 = [(S["act"], ev)]
                                    if sb == 0 and ("O", qt) in bguard:
                                        wv3.append(bguard[("O", qt)])
                                    pv_last = P.emit("tensor", lambda E, qt=qt, sb=sb, hl=hl, qt_abs=qt_abs: E.matmul(
                                        psOq[qt][:, 0:VS],
                                        PT[:, (sb % 2) * 512 + qt * 128:(sb % 2) * 512 + (qt + 1) * 128],
                                        vt[:, sb * vs_tile + hl * VS:sb * vs_tile + (hl + 1) * VS],
                                        start=(sb == 0), stop=(sb == qt_abs)), inc=S["pe"],
                                        wait=wv3 if qt == max(0, sb - tsb * 4) else [])
                                    if sb == qt_abs:
                                        pv_stop[qt] = (S["pe"], pv_last)
                                bguard[("PT", sb % 2)] = (S["pe"], pv_last)
                            ovv = None
                            if "no_pv" in DBG:
                                nv = P.emit("vector", lambda E: E.memset(oq[:, 0:1], 0.0), inc=S["dve"])
                                bguard[("qc", par)] = (S["pe"], pv_last)
                                negm_guard[hl] = (S["pe"], pv_last)
                                return (S["dve"], nv)
                            for qt in range(4):
                                wv4 = [pv_stop[qt]]
                                if oq_free is not None:
                                    wv4.append(oq_free)
                                P.emit("vector", lambda E, qt=qt: E.tensor_scalar(
                                    out=stats[:, 8 + qt:9 + qt],
                                    in0=psOq[qt][:, VW:VS],
                                    scalar1=1e-12, scalar2=None, op0=ALU.max), wait=wv4)
                                P.emit("vector", lambda E, qt=qt: E.reciprocal(
                                    out=stats[:, 12 + qt:13 + qt], in_=stats[:, 8 + qt:9 + qt]))
                                ovv = P.emit("vector", lambda E, qt=qt, hl=hl: E.tensor_scalar(
                                    out=oq[:, qt * GW + hl * VW:qt * GW + (hl + 1) * VW],
                                    in0=psOq[qt][:, 0:VW],
                                    scalar1=stats[:, 12 + qt:13 + qt], scalar2=None,
                                    op0=ALU.mult), inc=S["dve"])
                                bguard[("O", qt)] = (S["dve"], ovv)
                            bguard[("qc", par)] = (S["pe"], pv_last)
                            negm_guard[hl] = (S["pe"], pv_last)
                            return (S["dve"], ovv)

                        for tsb in range(nTSB):
                            if 'A' in hammer:
                                barrier()
                            pend = {}
                            pend[0] = mprep(0, tsb)
                            if 'B' in hammer:
                                barrier()
                            if HPG > 1:
                                pend[1] = mprep(1, tsb)
                                if 'C' in hammer:
                                    barrier()
                            o_done = []
                            for i in range(HPG):
                                o_done.append(sloop(i, tsb, pend[i]))
                                if 'D' in hammer:
                                    barrier()
                                if i + 2 < HPG:
                                    pend[i + 2] = mprep(i + 2, tsb)
                                    if 'E' in hammer:
                                        barrier()
                            if "no_outproj" in DBG:
                                continue
                            tp_last = None
                            tv_last = None
                            for qt in range(4):
                                if 'F' in hammer:
                                    barrier()
                                qt_abs = tsb * 4 + qt
                                xpar = qt % 2
                                xsrc = x_in if g == 0 else x_out
                                wv = o_done[-2:]
                                if ("xq", xpar) in bguard:
                                    wv.append(bguard[("xq", xpar)])
                                lv = P.emit("sync", lambda E, qt_abs=qt_abs, xpar=xpar, xsrc=xsrc: E.dma_start(
                                    out=xq[:, xpar * 1024:(xpar + 1) * 1024],
                                    in_=xsrc[qt_abs * 128:(qt_abs + 1) * 128, :]),
                                    inc=S["lx"],
                                    wait=wv + ([(S["st"], S["st"].n)] if g == 1 else []))
                                for oc in range(4):
                                    wv2 = list(o_done[-2:]) if oc == 0 else []
                                    if "ot" in bguard:
                                        wv2.append(bguard["ot"])
                                    if ("otb",) in bguard and oc == 0:
                                        wv2.append(bguard[("otb",)])
                                    tv = P.emit("tensor", lambda E, oc=oc, qt=qt: E.transpose(
                                        pstT[:], oq[:, qt * GW + oc * 128:qt * GW + (oc + 1) * 128],
                                        ident_bf[:]), inc=S["pe"], wait=wv2)
                                    evv = P.emit("scalar", lambda E, oc=oc: E.copy(
                                        out=otb[:, oc * 128:(oc + 1) * 128], in_=pstT[:]),
                                        inc=S["act"], wait=[(S["pe"], tv)])
                                    bguard["ot"] = (S["act"], evv)
                                    tp_last = (S["act"], evv)
                                    tv_last = tv
                                if "op_tponly" in DBG:
                                    bguard[("otb",)] = (S["pe"], tv_last)
                                    sv = P.emit("scalar", lambda E, qt_abs=qt_abs, xpar=xpar: E.dma_start(
                                        out=x_out[qt_abs * 128:(qt_abs + 1) * 128, :],
                                        in_=xq[:, xpar * 1024:(xpar + 1) * 1024]),
                                        inc=S["st"], wait=[(S["lx"], lv), tp_last], dma=True)
                                    bguard[("xq", xpar)] = (S["st"], sv)
                                    continue
                                for half in range(2):
                                    bank = half
                                    pp = psS[bank]
                                    wv3 = [tp_last, wo_ready_g]
                                    if ("S", bank) in bguard:
                                        wv3.append(bguard[("S", bank)])
                                    mv = None
                                    for oc in range(4):
                                        mv = P.emit("tensor", lambda E, pp=pp, oc=oc, half=half: E.matmul(
                                            pp[:], otb[:, oc * 128:(oc + 1) * 128],
                                            wot_g[:, oc * 1024 + half * 512:oc * 1024 + (half + 1) * 512],
                                            start=(oc == 0), stop=(oc == 3)), inc=S["pe"],
                                            wait=wv3 if oc == 0 else [])
                                    if "op_nomm" in DBG:
                                        mv = P.emit("tensor", lambda E: E.nop(), inc=S["pe"],
                                                    wait=wv3)
                                    if "op_noadd" in DBG or ("op_add0" in DBG and half == 1) or ("op_add1" in DBG and half == 0):
                                        av = P.emit("vector", lambda E, xpar=xpar: E.memset(
                                            xq[:, xpar * 1024:xpar * 1024 + 1], 0.0), inc=S["dve"],
                                            wait=[(S["pe"], mv), (S["lx"], lv)])
                                    else:
                                        pcv = P.emit("scalar", lambda E, pp=pp, half=half: E.copy(
                                            out=ptmp[:, half * 512:(half + 1) * 512], in_=pp[:]),
                                            inc=S["act"], wait=[(S["pe"], mv)] + ([aw_guard[half]] if half in aw_guard else []))
                                        av = P.emit("vector", lambda E, xpar=xpar, half=half: E.tensor_tensor(
                                            out=xq[:, xpar * 1024 + half * 512:xpar * 1024 + (half + 1) * 512],
                                            in0=xq[:, xpar * 1024 + half * 512:xpar * 1024 + (half + 1) * 512],
                                            in1=ptmp[:, half * 512:(half + 1) * 512], op=ALU.add),
                                            inc=S["dve"],
                                            wait=[(S["act"], pcv), (S["lx"], lv)])
                                        aw_guard[half] = (S["dve"], av)
                                    bguard[("S", bank)] = (S["dve"], av)
                                bguard[("otb",)] = (S["pe"], mv)
                                sv = P.emit("scalar", lambda E, qt_abs=qt_abs, xpar=xpar: E.dma_start(
                                    out=x_out[qt_abs * 128:(qt_abs + 1) * 128, :],
                                    in_=xq[:, xpar * 1024:(xpar + 1) * 1024]),
                                    inc=S["st"], wait=[bguard[("S", 1)]], dma=True)
                                bguard[("xq", xpar)] = (S["st"], sv)
                            if 'G' in hammer:
                                barrier()
                            oq_free = (S["pe"], tv_last)

                for g in range(NG):
                    passA(g)
                    barrier()
                    if "passA_only" not in DBG:
                        passB(g)
                    barrier()

        cur = x0
        pool = [xa, xb]
        for i, sname in enumerate(seq):
            dst = out_d if i == len(seq) - 1 else pool[i % 2]
            if sname == "f0":
                emit_ffn("l0", cur, dst)
            elif sname == "f1":
                emit_ffn("l1", cur, dst)
            elif sname == "a0":
                emit_attn("l0", cur, dst)
            elif sname == "a1":
                emit_attn("l1", cur, dst)
            barrier()
            cur = dst

        P.waits("scalar", (S["st"], S["st"].n))
        P.waits("sync", (S["st"], S["st"].n))

        with nc.Block() as block:
            @block.sync
            def _(E):
                for f in P.q["sync"]:
                    f(E)

            @block.tensor
            def _(E):
                for f in P.q["tensor"]:
                    f(E)

            @block.vector
            def _(E):
                for f in P.q["vector"]:
                    f(E)

            @block.scalar
            def _(E):
                for f in P.q["scalar"]:
                    f(E)

            @block.gpsimd
            def _(E):
                for f in P.q["gpsimd"]:
                    f(E)
    return nc


def make_consts():
    ident_bf = np.eye(128, dtype=ml_dtypes.bfloat16)
    ident_f = np.eye(128, dtype=np.float32)
    cmask = np.zeros((128, 2048), np.float32)
    for k in range(4):
        p = np.arange(128)[:, None]
        j = np.arange(512)[None, :]
        cmask[:, k * 512:(k + 1) * 512] = np.where(k * 128 + p > j, NEG, 0.0).astype(np.float32)
    return {"ident_bf": ident_bf, "ident_f": ident_f, "cmask": cmask}


def kernel(**inputs):
    T = 4096
    nc = build(T=T, hammer="ABC")
    consts = make_consts()
    x = np.ascontiguousarray(inputs["L0_attn_x"], dtype=np.float32)
    base = {k: np.ascontiguousarray(v) for k, v in inputs.items() if k != "L0_attn_x"}
    base.update(consts)
    in_maps = []
    for c in range(8):
        m = dict(base)
        m["x0"] = x[c]
        in_maps.append(m)
    res = run_bass_kernel_spmd(nc, in_maps, core_ids=list(range(8)))
    return np.stack([np.asarray(res.results[c]["out"], dtype=np.float32) for c in range(8)])
